# version 10
# baseline (speedup 1.0000x reference)
import numpy as np
import ml_dtypes
from contextlib import ExitStack

import concourse.bass as bass
import concourse.mybir as mybir
from concourse.bass_utils import run_bass_kernel_spmd

F32 = mybir.dt.float32
BF16 = mybir.dt.bfloat16
AF = mybir.ActivationFunctionType
ALU = mybir.AluOpType
AX = mybir.AxisListType
BF = ml_dtypes.bfloat16
AP = bass.AP
ds = bass.ds

D = 1024
H = 16
HD = 64
SCALE = HD ** -0.5
LAM = 64.0
EPS = 1e-5
NE = 8
HID = 4096
CRED = 64
CREDD = 1 << 18


def build_nc(S=4096, dbg=False):
    NT = S // 128
    NT2 = NT // 2
    NQC = S // 512
    NQC2 = max(NQC // 2, 1)
    nc = bass.Bass("TRN2", target_bir_lowering=False)

    x_h = nc.dram_tensor("x", [S, D], F32, kind="ExternalInput")
    wq1t_h = nc.dram_tensor("wq1t", [D, 3 * D], F32, kind="ExternalInput")
    woutt_h = nc.dram_tensor("woutt", [D, D], F32, kind="ExternalInput")
    bout_h = nc.dram_tensor("bout", [1, D], F32, kind="ExternalInput")
    ln1w_h = nc.dram_tensor("ln1w", [1, D], F32, kind="ExternalInput")
    ln1b_h = nc.dram_tensor("ln1b", [1, D], F32, kind="ExternalInput")
    ln2w_h = nc.dram_tensor("ln2w", [1, D], F32, kind="ExternalInput")
    ln2b_h = nc.dram_tensor("ln2b", [1, D], F32, kind="ExternalInput")
    wq2_h = nc.dram_tensor("wq2", [D, 3 * D], F32, kind="ExternalInput")
    wproj_h = nc.dram_tensor("wproj", [D, D], F32, kind="ExternalInput")
    gatew_h = nc.dram_tensor("gatew", [D, NE], F32, kind="ExternalInput")
    gateb_h = nc.dram_tensor("gateb", [1, NE], F32, kind="ExternalInput")
    w1_h = nc.dram_tensor("w1", [NE, D, HID], F32, kind="ExternalInput")
    w2_h = nc.dram_tensor("w2", [NE, HID, D], F32, kind="ExternalInput")
    cosr_h = nc.dram_tensor("cosr", [S, 512], BF16, kind="ExternalInput")
    sinr_h = nc.dram_tensor("sinr", [S, 512], BF16, kind="ExternalInput")
    ident_h = nc.dram_tensor("identh", [128, 128], BF16, kind="ExternalInput")
    y_h = nc.dram_tensor("y", [S, D], F32, kind="ExternalOutput")

    kd = dict(kind="ExternalOutput") if dbg else {}
    q1tp_h = nc.dram_tensor("q1tp", [D, S], BF16, **kd)
    mu1_h = nc.dram_tensor("mu1", [H, S], BF16, **kd)
    kaug1_h = nc.dram_tensor("kaug1", [H, HD + 1, S], BF16, **kd)
    attnt1_h = nc.dram_tensor("attnt1", [D, S], BF16, **kd)
    x1_h = nc.dram_tensor("x1s", [S, D], F32, **kd)
    ln1x_h = nc.dram_tensor("ln1xs", [S, D], BF16, **kd)
    q2tp_h = nc.dram_tensor("q2tp", [D, S], BF16, **kd)
    attnt2_h = nc.dram_tensor("attnt2", [D, S], BF16, **kd)
    u_h = nc.dram_tensor("us", [S, D], F32, **kd)
    bt_h = nc.dram_tensor("bts", [D, S], BF16, **kd)
    w1b_h = nc.dram_tensor("w1b", [NE, D, HID], BF16)
    w2b_h = nc.dram_tensor("w2b", [NE, HID, D], BF16)
    mh_h = nc.dram_tensor("mhs", [2, 16], F32, **(dict(kind="ExternalOutput") if dbg else {}))
    masks_h = nc.dram_tensor("masksh", [NE, S], F32, **(dict(kind="ExternalOutput") if dbg else {}))
    dbg1_h = nc.dram_tensor("dbg1", [128, 192], F32, kind="ExternalOutput") if dbg else None
    dbg2_h = nc.dram_tensor("dbg2", [128, 1032], F32, kind="ExternalOutput") if dbg else None
    dbg2_used = {"v": False}

    ctx = ExitStack()

    ident = ctx.enter_context(nc.sbuf_tensor("ident", [128, 128], BF16))
    onescol = ctx.enter_context(nc.sbuf_tensor("onescol", [1, 64], BF16))
    neg1row = ctx.enter_context(nc.sbuf_tensor("neg1row", [1, 512], BF16))
    mb1 = ctx.enter_context(nc.sbuf_tensor("mb1", [128, 16], F32))
    mbr = ctx.enter_context(nc.sbuf_tensor("mbr", [128, 16], F32))
    mb2 = ctx.enter_context(nc.sbuf_tensor("mb2", [128, 16], F32))
    zeros = ctx.enter_context(nc.sbuf_tensor("zeros", [128, 16], F32))
    epsc = ctx.enter_context(nc.sbuf_tensor("epsc", [128, 1], F32))
    invd = ctx.enter_context(nc.sbuf_tensor("invd", [128, 1], F32))

    from concourse.bass import OrderedEngineSet
    from contextlib import contextmanager

    @contextmanager
    def fori(eng, n):
        engines = OrderedEngineSet([eng.engine])
        name = f"ff{nc.next_id()}"
        ls, le = f"{name}_l", f"{name}_e"
        regs = nc.alloc_registers(f"{name}_i", engines=engines)
        nc.regs_mov(regs, 0)
        nc.br(ls, engines=engines)
        with nc.body(ls, valid_engines=engines):
            yield list(regs)[0]
            nc.regs_alu(regs, regs, 1, op=ALU.add)
            nc.br_lt(regs, n, on_true=ls, on_false=le, engines=engines)
        nc.switch_bb(le)
        for r in regs:
            eng.free_register(r)

    def rcalc(eng, w, i, mult, const):
        eng.reg_alu(w, i, mult, ALU.mult)
        if const > 0:
            eng.reg_alu(w, w, const, ALU.add)
        elif const < 0:
            eng.reg_alu(w, w, -const, ALU.subtract)
        return w

    def rwait(eng, w, semh, i, mult, const):
        eng.wait_ge(semh, rcalc(eng, w, i, mult, const))

    def sem(name):
        return ctx.enter_context(nc.semaphore(name))
    sL = sem("sL")
    sQ = sem("sQ")
    sO = sem("sO")
    sWc = sem("sWc")
    sOc = sem("sOc")
    cT = sem("cT")
    cV = sem("cV")
    cA = sem("cA")
    cG = sem("cG")
    L = {"sL": CREDD, "sQ": CREDD, "sO": CREDD, "sWc": 0, "sOc": 0,
         "cT": CRED, "cV": CRED, "cA": CRED, "cG": CRED}

    CW = 1024
    wstg = ctx.enter_context(nc.sbuf_tensor("wstg", [128, 4 * CW], BF16))
    cpub = ctx.enter_context(nc.sbuf_tensor("cpub", [1, 1], F32))
    sCl = [sem(f"sCl{k}") for k in range(4)]
    sCs = [sem(f"sCs{k}") for k in range(4)]
    cast_jobs = []
    for e in range(NE):
        for dc in range(8):
            for quar in range(4):
                cast_jobs.append((w1_h, w1b_h, (e * D + dc * 128) * HID + quar * CW, HID))
        for hc in range(32):
            cast_jobs.append((w2_h, w2b_h, (e * HID + hc * 128) * D, CW))
    EJOBS = len(cast_jobs) // NE
    cst = {"i": 0}

    def _cast_store(gpsimd, j):
        src_t, dst_t, off, stride = cast_jobs[j]
        k = j % 4
        gpsimd.wait_ge(sCl[k], 16 * (j // 4 + 1))
        gpsimd.dma_start(AP(dst_t, off, [[stride, 128], [1, CW]]),
                         wstg[:, k * CW:(k + 1) * CW]).then_inc(sCs[k], 16)
        if j % EJOBS == EJOBS - 1:
            for kk in range(4):
                gpsimd.wait_ge(sCs[kk], 16 * ((j + 1) // 4))
            gpsimd.dma_start(cpub[0:1, 0:1], zeros[0:1, 0:1]).then_inc(sOc, 16)

    def emit_casts(gpsimd, n):
        for _ in range(n):
            i = cst["i"]
            if i >= len(cast_jobs):
                return
            cst["i"] += 1
            src_t, dst_t, off, stride = cast_jobs[i]
            k = i % 4
            if i >= 4:
                gpsimd.wait_ge(sCs[k], 16 * (i // 4))
            gpsimd.dma_start(wstg[:, k * CW:(k + 1) * CW],
                             AP(src_t, off, [[stride, 128], [1, CW]])).then_inc(sCl[k], 16)
            if i >= 2:
                _cast_store(gpsimd, i - 2)

    def finish_casts(gpsimd):
        emit_casts(gpsimd, len(cast_jobs))
        for j in range(max(len(cast_jobs) - 2, 0), len(cast_jobs)):
            _cast_store(gpsimd, j)
        for k in range(4):
            njk = (len(cast_jobs) - 1 - k) // 4 + 1
            gpsimd.wait_ge(sCs[k], 16 * njk)

    with nc.Block() as blk:
        @blk.sync
        def _(sync):
            sync.dma_start(ident[:], ident_h[:, :]).then_inc(sL, 16)
            sync.wait_ge(sL, 16)

        @blk.vector
        def _(vector):
            vector.memset(onescol[:], 1.0).then_inc(cV, CRED)
            vector.memset(neg1row[:], -1.0).then_inc(cT, CRED)
            vector.memset(zeros[:], 0.0).then_inc(cA, CRED)
            vector.memset(epsc[:], EPS).then_inc(cG, CRED)
            vector.memset(invd[:], 1.0 / D)
            vector.memset(zeros[:, 0:1], 0.0).then_inc(sL, CREDD)
            vector.memset(zeros[:, 0:1], 0.0).then_inc(sQ, CREDD)
            vector.memset(zeros[:, 0:1], 0.0).then_inc(sO, CREDD)
    L["sL"] += 16
    nc.all_engine_barrier()

    with nc.Block() as blk:
        @blk.sync
        def _(sync):
            for h in range(H):
                for j in range(S // 512):
                    sync.dma_start(AP(kaug1_h, (h * (HD + 1) + HD) * S + j * 512,
                                      [[1, 1], [1, 512]]),
                                   neg1row[:, :]).then_inc(sO, 16)
            L["sO"] += 16 * H * (S // 512)
            sync.wait_ge(sO, L["sO"])
    nc.all_engine_barrier()

    PH = {"n": 0}

    def uname(base):
        PH["n"] += 1
        return f"{base}{PH['n']}"

    def kvq_phase(mode, rope, xin_h, xin_f32, w_sb, wcol0, wrow, kts=None, vaugT=None,
                  kaug=None, qtp=None, n2_sb=None, n2col=0):
        grp = ExitStack()
        xin = grp.enter_context(nc.sbuf_tensor(uname("xin"), [128, 2 * 1024], F32)) if xin_f32 else None
        xbf = grp.enter_context(nc.sbuf_tensor(uname("xbf"), [128, 2 * 1024], BF16))
        xtt = grp.enter_context(nc.sbuf_tensor(uname("xtt"), [128, 2 * 1024], BF16))
        rot = grp.enter_context(nc.sbuf_tensor(uname("rot"), [128, 2 * 1024], BF16))
        rtmp = grp.enter_context(nc.sbuf_tensor(uname("rtmp"), [128, 1024], BF16))
        cosb = grp.enter_context(nc.sbuf_tensor(uname("cosb"), [128, 2 * 512], BF16)) if rope else None
        sinb = grp.enter_context(nc.sbuf_tensor(uname("sinb"), [128, 2 * 512], BF16)) if rope else None
        sqd = grp.enter_context(nc.sbuf_tensor(uname("sqd"), [128, 1024], BF16))
        hsum = grp.enter_context(nc.sbuf_tensor(uname("hsum"), [128, 16], F32))
        qtt = grp.enter_context(nc.sbuf_tensor(uname("qtt"), [128, 2 * 1024], BF16)) if mode == "q" else None
        ktstage = grp.enter_context(nc.sbuf_tensor(uname("ktst"), [128, 2 * 1024], BF16)) if mode == "kv" else None
        vstage = grp.enter_context(nc.sbuf_tensor(uname("vst"), [128, 2 * 1024], BF16)) if mode == "kv" else None
        nfeat = 2048 if mode == "kv" else 1024
        ps = grp.enter_context(nc.psum_tensor(uname("pskvq"), [128, 2048], F32))
        tps = grp.enter_context(nc.psum_tensor(uname("tpskvq"), [128, 2048], BF16))
        base = dict(L)
        VPT, TPT, APT = 5, 3, 1
        DPT = 16 * (3 if rope else 1)
        if mode == "kv":
            DOPT = 16 * (9 + (16 if kaug is not None else 0))
        else:
            DOPT = 16
        V0, T0, A0, L0 = base["cV"], base["cT"], base["cA"], base["sL"]
        O00 = base["sO"]

        with nc.Block() as blk:
            @blk.sync
            def _(sync):
                for t in range(NT):
                    par = t % 2
                    if t >= 2:
                        sync.wait_ge(cV, V0 + VPT * (t - 1))
                    dst = (xin if xin_f32 else xbf)[:, par * 1024:(par + 1) * 1024]
                    sync.dma_start(dst, AP(xin_h, t * 128 * D, [[D, 128], [1, D]])
                                   ).then_inc(sL, 16)
                    if rope:
                        sync.dma_start(cosb[:, par * 512:(par + 1) * 512],
                                       AP(cosr_h, t * 128 * 512, [[512, 128], [1, 512]])
                                       ).then_inc(sL, 16)
                        sync.dma_start(sinb[:, par * 512:(par + 1) * 512],
                                       AP(sinr_h, t * 128 * 512, [[512, 128], [1, 512]])
                                       ).then_inc(sL, 16)
                    sync.wait_ge(cV, V0 + VPT * t + 5)
                    if mode == "kv":
                        for c in range(8):
                            sync.dma_start(kts[c][:, t * 128:(t + 1) * 128],
                                           ktstage[:, par * 1024 + c * 128:
                                                   par * 1024 + (c + 1) * 128]
                                           ).then_inc(sO, 16)
                        sync.dma_start(AP(vaugT, t * 1040,
                                          [[NT * 1040, 128], [65, 16], [1, 64]]),
                                       vstage[:, par * 1024:(par + 1) * 1024]
                                       ).then_inc(sO, 16)
                        if kaug is not None:
                            for f in range(8):
                                sync.dma_start(AP(kaug, (2 * f) * (HD + 1) * S + t * 128,
                                                  [[S, 64], [1, 128]]),
                                               ktstage[0:64, par * 1024 + f * 128:
                                                       par * 1024 + (f + 1) * 128]
                                               ).then_inc(sO, 16)
                                sync.dma_start(AP(kaug, (2 * f + 1) * (HD + 1) * S + t * 128,
                                                  [[S, 64], [1, 128]]),
                                               ktstage[64:128, par * 1024 + f * 128:
                                                       par * 1024 + (f + 1) * 128]
                                               ).then_inc(sO, 16)
                    if mode == "q":
                        sync.dma_start(AP(qtp, t * 128, [[S, 128], [128 * S, 8], [1, 128]]),
                                       qtt[:, par * 1024:(par + 1) * 1024]).then_inc(sO, 16)

            @blk.vector
            def _(vector):
              with vector.register(uname("rw")) as w:
                with fori(vector, NT2) as it:
                    for par in range(2):
                        rwait(vector, w, sL, it, 2 * DPT, par * DPT + L0 + DPT)
                        rwait(vector, w, sO, it, 2 * DOPT, par * DOPT + O00 - DOPT)
                        if xin_f32:
                            vector.tensor_copy(xbf[:, par * 1024:(par + 1) * 1024],
                                               xin[:, par * 1024:(par + 1) * 1024]
                                               ).then_inc(cV, 1)
                        else:
                            vector.memset(hsum[:, 0:1], 0.0).then_inc(cV, 1)
                        rwait(vector, w, cT, it, 2 * TPT, par * TPT + T0 + 1)
                        vector.tensor_copy(xtt[:, par * 1024:(par + 1) * 1024],
                                           tps[:, par * 1024:(par + 1) * 1024]
                                           ).then_inc(cV, 1)
                        rwait(vector, w, cT, it, 2 * TPT, par * TPT + T0 + 2)
                        if rope:
                            x1v = AP(ps, 0, [[2048, 128], [64, 16], [1, 32]])
                            x2v = AP(ps, 32, [[2048, 128], [64, 16], [1, 32]])
                            cv_ = AP(cosb, par * 512, [[1024, 128], [32, 16], [1, 32]])
                            sv_ = AP(sinb, par * 512, [[1024, 128], [32, 16], [1, 32]])
                            t1 = AP(rtmp, 0, [[1024, 128], [64, 16], [1, 32]])
                            t2 = AP(rtmp, 32, [[1024, 128], [64, 16], [1, 32]])
                            r1 = AP(rot, par * 1024, [[2048, 128], [64, 16], [1, 32]])
                            r2 = AP(rot, par * 1024 + 32, [[2048, 128], [64, 16], [1, 32]])
                            vector.tensor_mul(t1, x1v, sv_)
                            vector.tensor_mul(t2, x2v, sv_)
                            vector.tensor_mul(r1, x1v, cv_)
                            vector.tensor_tensor(r1, r1, t2, ALU.subtract)
                            vector.tensor_mul(r2, x2v, cv_)
                            vector.tensor_tensor(r2, r2, t1, ALU.add)
                        else:
                            vector.tensor_copy(rot[:, par * 1024:(par + 1) * 1024],
                                               ps[:, 0:1024])
                        if mode == "kv":
                            vector.tensor_copy(vstage[:, par * 1024:(par + 1) * 1024],
                                               ps[:, 1024:2048])
                        vector.memset(hsum[:, 0:1], 0.0).then_inc(cV, 1)
                        rwait(vector, w, cA, it, 2 * APT, par * APT + A0 + 1)
                        vector.tensor_reduce(hsum[:, 0:16],
                                             AP(sqd, 0, [[1024, 128], [64, 16], [1, 64]]),
                                             AX.X, ALU.add).then_inc(cV, 1)
                        rwait(vector, w, cV, it, 2 * VPT, par * VPT + V0 + 4)
                        vector.tensor_tensor(n2_sb[:, n2col:n2col + 16],
                                             n2_sb[:, n2col:n2col + 16],
                                             hsum[:, 0:16], ALU.max)
                        rwait(vector, w, cT, it, 2 * TPT, par * TPT + T0 + 3)
                        if mode == "kv":
                            vector.tensor_copy(ktstage[:, par * 1024:(par + 1) * 1024],
                                               tps[:, par * 1024:(par + 1) * 1024]
                                               ).then_inc(cV, 1)
                        else:
                            vector.tensor_copy(qtt[:, par * 1024:(par + 1) * 1024],
                                               tps[:, par * 1024:(par + 1) * 1024]
                                               ).then_inc(cV, 1)

            @blk.scalar
            def _(scalar):
              with scalar.register(uname("rw")) as w:
                with fori(scalar, NT2) as it:
                    for par in range(2):
                        rwait(scalar, w, cV, it, 2 * VPT, par * VPT + V0 + 3)
                        scalar.activation(sqd[:], rot[:, par * 1024:(par + 1) * 1024],
                                          AF.Square, bias=0.0, scale=1.0).then_inc(cA, 1)

            @blk.tensor
            def _(tensor):
              with tensor.register(uname("rw")) as w:
                with fori(tensor, NT2) as it:
                    for par in range(2):
                        rwait(tensor, w, cV, it, 2 * VPT, par * VPT + V0 + 1)
                        for c in range(8):
                            ins = tensor.transpose(
                                tps[:, par * 1024 + c * 128:par * 1024 + (c + 1) * 128],
                                xbf[:, par * 1024 + c * 128:par * 1024 + (c + 1) * 128],
                                ident[:])
                        ins.then_inc(cT, 1)
                        rwait(tensor, w, cV, it, 2 * VPT, par * VPT + V0 + 2)
                        for n in range(nfeat // 512):
                            for c in range(8):
                                ins = tensor.matmul(
                                    ps[:, n * 512:(n + 1) * 512],
                                    xtt[:, par * 1024 + c * 128:par * 1024 + (c + 1) * 128],
                                    w_sb[:, c * wrow + wcol0 + n * 512:
                                         c * wrow + wcol0 + (n + 1) * 512],
                                    start=(c == 0), stop=(c == 7))
                        ins.then_inc(cT, 1)
                        rwait(tensor, w, cV, it, 2 * VPT, par * VPT + V0 + 3)
                        for c in range(8):
                            ins = tensor.transpose(
                                tps[:, par * 1024 + c * 128:par * 1024 + (c + 1) * 128],
                                rot[:, par * 1024 + c * 128:par * 1024 + (c + 1) * 128],
                                ident[:])
                        ins.then_inc(cT, 1)

            @blk.gpsimd
            def _(gpsimd):
                for t in range(0, NT, 2):
                    gpsimd.wait_ge(cV, V0 + VPT * t)
                    emit_casts(gpsimd, 2)

        L["sL"] = L0 + DPT * NT
        L["cV"] = V0 + VPT * NT
        L["cT"] = T0 + TPT * NT
        L["cA"] = A0 + APT * NT
        L["sO"] = O00 + DOPT * NT
        nc.all_engine_barrier()
        grp.close()

    def norm_finalize(n2_sb, slot):
        grp = ExitStack()
        nbf = grp.enter_context(nc.sbuf_tensor(uname("nbf"), [128, 32], BF16))
        ntr = grp.enter_context(nc.sbuf_tensor(uname("ntr"), [16, 256], F32))
        nred = grp.enter_context(nc.sbuf_tensor(uname("nred"), [16, 8], F32))
        tpsn = grp.enter_context(nc.psum_tensor(uname("tpsn"), [16, 256], BF16))
        base = dict(L)
        with nc.Block() as blk:
            @blk.vector
            def _(vector):
                vector.tensor_copy(nbf[:], n2_sb[:, 0:32]).then_inc(cV, 1)
                vector.wait_ge(cT, base["cT"] + 1)
                vector.tensor_copy(ntr[:, 0:128], tpsn[0:16, 0:128])
                vector.tensor_copy(ntr[:, 128:256], tpsn[0:16, 128:256]).then_inc(cV, 1)
                vector.reduce_max(nred[:, 0:1], ntr[:, 0:128], AX.X).then_inc(cV, 1)
                vector.reduce_max(nred[:, 1:2], ntr[:, 128:256], AX.X).then_inc(cV, 1)
                vector.wait_ge(cA, base["cA"] + 1)
                vector.tensor_mul(nred[:, 4:5], nred[:, 2:3], nred[:, 3:4]).then_inc(cV, 1)

            @blk.tensor
            def _(tensor):
                tensor.wait_ge(cV, base["cV"] + 1)
                tensor.transpose(tpsn[0:16, 0:128], nbf[:, 0:16], ident[:])
                tensor.transpose(tpsn[0:16, 128:256], nbf[:, 16:32], ident[:]).then_inc(cT, 1)

            @blk.scalar
            def _(scalar):
                scalar.wait_ge(cV, base["cV"] + 4)
                scalar.activation(nred[:, 2:3], nred[:, 0:1], AF.Sqrt, bias=0.0, scale=1.0)
                scalar.activation(nred[:, 3:4], nred[:, 1:2], AF.Sqrt, bias=0.0,
                                  scale=1.0).then_inc(cA, 1)

            @blk.sync
            def _(sync):
                sync.wait_ge(cV, base["cV"] + 5)
                sync.dma_start(AP(mh_h, slot * 16, [[1, 16]]), nred[0:16, 4:5]
                               ).then_inc(sO, 16)
                sync.wait_ge(sO, base["sO"] + 16)
                sync.dma_start(mbr[:] if slot == 0 else mb2[:],
                               AP(mh_h, slot * 16, [[0, 128], [1, 16]])).then_inc(sL, 16)
                sync.wait_ge(sL, base["sL"] + 16)
        L["cV"] = base["cV"] + 5
        L["cT"] = base["cT"] + 1
        L["cA"] = base["cA"] + 1
        L["sO"] = base["sO"] + 16
        L["sL"] = base["sL"] + 16
        nc.all_engine_barrier()
        with nc.Block() as blk:
            @blk.vector
            def _(vector):
                if slot == 0:
                    vector.tensor_scalar_mul(mb1[:], mbr[:], -1.0 / LAM)
                else:
                    vector.tensor_scalar_mul(mb2[:], mb2[:], -SCALE)
        nc.all_engine_barrier()
        grp.close()

    s1 = ExitStack()
    kts1 = [s1.enter_context(nc.sbuf_tensor(f"k1t{f}", [128, S], BF16)) for f in range(8)]
    vaug1 = s1.enter_context(nc.sbuf_tensor(uname("vaug1"), [128, NT * 1040], BF16))
    n2s1 = s1.enter_context(nc.sbuf_tensor(uname("n2s1"), [128, 32], F32))

    def load_w_chunks(w_sb, src_h, col0, width):
        with nc.Block() as blk:
            @blk.gpsimd
            def _(gpsimd):
                for c in range(8):
                    gpsimd.dma_start(w_sb[:, c * width:(c + 1) * width],
                                     AP(src_h, c * 128 * 3 * D + col0,
                                        [[3 * D, 128], [1, width]])).then_inc(sWc, 16)
                L["sWc"] += 16 * 8
                gpsimd.wait_ge(sWc, L["sWc"])
        nc.all_engine_barrier()

    with nc.Block() as blk:
        @blk.vector
        def _(vector):
            vector.memset(AP(vaug1, 64, [[NT * 1040, 128], [65, 16 * NT], [1, 1]]), 1.0)
            vector.memset(n2s1[:], 0.0)
    nc.all_engine_barrier()

    wkv_grp = ExitStack()
    wkv1 = wkv_grp.enter_context(nc.sbuf_tensor(uname("wkv1"), [128, 8 * 2048], BF16))
    load_w_chunks(wkv1, wq1t_h, D, 2048)
    kvq_phase("kv", True, x_h, True, wkv1, 0, 2048, kts=kts1, vaugT=vaug1,
              kaug=kaug1_h, n2_sb=n2s1, n2col=16)
    wkv_grp.close()
    wq_grp = ExitStack()
    wqq1 = wq_grp.enter_context(nc.sbuf_tensor(uname("wqq1"), [128, 8 * 1024], BF16))
    load_w_chunks(wqq1, wq1t_h, 0, 1024)
    kvq_phase("q", True, x_h, True, wqq1, 0, 1024, qtp=q1tp_h, n2_sb=n2s1, n2col=0)
    wq_grp.close()
    norm_finalize(n2s1, 0)

    p1 = ExitStack()
    qbuf = p1.enter_context(nc.sbuf_tensor(uname("qbuf"), [128, 2 * 1024], BF16))
    dump = p1.enter_context(nc.sbuf_tensor(uname("dump"), [128, 1024], BF16))
    dacc = p1.enter_context(nc.sbuf_tensor(uname("dacc"), [128, 16 * NQC], F32))
    dsum = p1.enter_context(nc.sbuf_tensor(uname("dsum"), [128, 16], F32))
    lnd = p1.enter_context(nc.sbuf_tensor(uname("lnd"), [128, 16], F32))
    mutbf = p1.enter_context(nc.sbuf_tensor(uname("mutbf"), [128, 2 * 16], BF16))
    mutr = p1.enter_context(nc.sbuf_tensor(uname("mutr"), [16, 2 * 128], BF16))
    tpsm = p1.enter_context(nc.psum_tensor(uname("tpsm"), [16, 128], BF16))
    psp1 = p1.enter_context(nc.psum_tensor(uname("psp1"), [128, 2048], F32))
    base = dict(L)
    MMT = 8 * NQC
    T0, A0, V0, G0, L0 = base["cT"], base["cA"], base["cV"], base["cG"], base["sL"]
    O0 = base["sO"]
    with nc.Block() as blk:
        @blk.sync
        def _(sync):
            for t in range(NT):
                par = t % 2
                if t >= 2:
                    sync.wait_ge(cT, T0 + (MMT + 1) * (t - 1))
                sync.dma_start(qbuf[:, par * 1024:(par + 1) * 1024],
                               AP(q1tp_h, t * 128, [[S, 128], [128 * S, 8], [1, 128]])
                               ).then_inc(sL, 16)
                sync.wait_ge(cV, V0 + 3 * t + 3)
                sync.dma_start(AP(mu1_h, t * 128, [[S, 16], [1, 128]]),
                               mutr[0:16, par * 128:(par + 1) * 128]).then_inc(sO, 16)
            if dbg1_h is not None:
                sync.dma_start(AP(dbg1_h, 0, [[192, 128], [1, 128]]), dacc[:]).then_inc(sO, 16)
                sync.dma_start(AP(dbg1_h, 128, [[192, 128], [1, 16]]), dsum[:]).then_inc(sO, 16)
                sync.dma_start(AP(dbg1_h, 144, [[192, 128], [1, 16]]), lnd[:]).then_inc(sO, 16)
                sync.dma_start(AP(dbg1_h, 160, [[192, 128], [1, 16]]), lnd[:]).then_inc(sO, 16)
                L["sO"] += 64

        @blk.tensor
        def _(tensor):
          with tensor.register(uname("rw")) as w:
            with fori(tensor, NT2) as it:
                for par in range(2):
                    rwait(tensor, w, sL, it, 32, par * 16 + L0 + 16)
                    for b in range(8):
                        for kc in range(NQC):
                            i = b * NQC + kc
                            bk = (i % 2) * 2
                            rwait(tensor, w, cA, it, 2 * MMT, par * MMT + A0 + i - 1)
                            tensor.matmul(psp1[:, bk * 512:(bk + 1) * 512],
                                          qbuf[0:64, par * 1024 + b * 128:par * 1024 + (b + 1) * 128],
                                          kts1[b][0:64, kc * 512:(kc + 1) * 512],
                                          start=True, stop=True)
                            tensor.matmul(psp1[:, (bk + 1) * 512:(bk + 2) * 512],
                                          qbuf[64:128, par * 1024 + b * 128:par * 1024 + (b + 1) * 128],
                                          kts1[b][64:128, kc * 512:(kc + 1) * 512],
                                          start=True, stop=True,
                                          tile_position=(64, 0)).then_inc(cT, 1)
                    rwait(tensor, w, cV, it, 6, par * 3 + V0 + 2)
                    tensor.transpose(tpsm[0:16, :], mutbf[:, par * 16:(par + 1) * 16],
                                     ident[:]).then_inc(cT, 1)

        @blk.scalar
        def _(scalar):
          with scalar.register(uname("rw")) as w:
            with fori(scalar, NT2) as it:
                for par in range(2):
                    rwait(scalar, w, cV, it, 6, par * 3 + V0 - 2)
                    scalar.memzero(dacc[:])
                    for b in range(8):
                        for kc in range(NQC):
                            i = b * NQC + kc
                            bk = (i % 2) * 2
                            rwait(scalar, w, cT, it, 2 * (MMT + 1), par * (MMT + 1) + T0 + i + 1)
                            scalar.activation(dump[:, 0:512], psp1[:, bk * 512:(bk + 1) * 512],
                                              AF.Exp, bias=mb1[:, 2 * b:2 * b + 1],
                                              scale=1.0 / LAM,
                                              accum_out=dacc[:, 2 * b * NQC + kc:
                                                             2 * b * NQC + kc + 1])
                            scalar.activation(dump[:, 0:512], psp1[:, (bk + 1) * 512:(bk + 2) * 512],
                                              AF.Exp, bias=mb1[:, 2 * b + 1:2 * b + 2],
                                              scale=1.0 / LAM,
                                              accum_out=dacc[:, (2 * b + 1) * NQC + kc:
                                                             (2 * b + 1) * NQC + kc + 1]
                                              ).then_inc(cA, 1)
                    rwait(scalar, w, cV, it, 6, par * 3 + V0 + 1)
                    scalar.activation(lnd[:], dsum[:], AF.Ln, bias=0.0, scale=1.0
                                      ).then_inc(cG, 1)

        @blk.vector
        def _(vector):
          with vector.register(uname("rw")) as w, vector.register(uname("ro")) as o:
            with fori(vector, NT2) as it:
                for par in range(2):
                    rwait(vector, w, cA, it, 2 * MMT, par * MMT + A0 + MMT)
                    for hh in range(16):
                        ins = vector.tensor_reduce(dsum[:, hh:hh + 1],
                                                   dacc[:, hh * NQC:(hh + 1) * NQC],
                                                   AX.X, ALU.add)
                    ins.then_inc(cV, 1)
                    rwait(vector, w, cG, it, 2, par + G0 + 1)
                    vector.scalar_tensor_tensor(mutbf[:, par * 16:(par + 1) * 16],
                                                lnd[:], LAM, mbr[:, 0:16],
                                                ALU.mult, ALU.add).then_inc(cV, 1)
                    rwait(vector, w, cT, it, 2 * (MMT + 1), par * (MMT + 1) + T0 + MMT + 1)
                    rwait(vector, w, sO, it, 32, par * 16 + O0 - 16)
                    vector.tensor_copy(mutr[0:16, par * 128:(par + 1) * 128],
                                       tpsm[0:16, :]).then_inc(cV, 1)

        @blk.gpsimd
        def _(gpsimd):
            for t in range(0, NT, 2):
                gpsimd.wait_ge(cV, V0 + 3 * t)
                emit_casts(gpsimd, 12)

    L["sL"] = L0 + 16 * NT
    L["sO"] = O0 + 16 * NT
    L["cT"] = T0 + (MMT + 1) * NT
    L["cA"] = A0 + MMT * NT
    L["cG"] = G0 + NT
    L["cV"] = V0 + 3 * NT
    nc.all_engine_barrier()

    p1.close()

    def pass2_unpacked():
        grp = ExitStack()
        kbuf = grp.enter_context(nc.sbuf_tensor(uname("kbuf"), [128, 2 * S], BF16))
        qaug = grp.enter_context(nc.sbuf_tensor(uname("qaug"), [128, 2 * 512], BF16))
        exps = grp.enter_context(nc.sbuf_tensor(uname("exps"), [128, 4 * 512], BF16))
        rcp = grp.enter_context(nc.sbuf_tensor(uname("rcp"), [1, 512], BF16))
        rb = grp.enter_context(nc.sbuf_tensor(uname("rb"), [64, 512], BF16))
        att = grp.enter_context(nc.sbuf_tensor(uname("att"), [64, 2 * 512], BF16))
        psc = grp.enter_context(nc.psum_tensor(uname("psc"), [128, 2048], F32))
        pao = grp.enter_context(nc.psum_tensor(uname("pao"), [128, 1536], F32))
        NKT = NT
        TQ = NKT + 1
        AQ = NKT // 2
        QPARS = [0, 1] if NQC >= 2 else [0]
        NQP = len(QPARS)
        NQI = NQC // NQP
        base0 = dict(L)
        for h in range(H):
            par = h % 2
            base = dict(L)
            T0, A0, V0, G0, Q0, O0 = (base["cT"], base["cA"], base["cV"],
                                      base["cG"], base["sQ"], base["sO"])
            with nc.Block() as blk:
                @blk.sync
                def _(sync):
                    if h >= 2:
                        sync.wait_ge(cT, base0["cT"] + TQ * NQC * (h - 1))
                    sync.dma_start(kbuf[0:65, par * S:par * S + S],
                                   AP(kaug1_h, h * (HD + 1) * S, [[S, 65], [1, S]])
                                   ).then_inc(sQ, 16)
                    for qc in range(NQC):
                        qpar = qc % 2
                        if qc >= 2:
                            sync.wait_ge(cT, T0 + TQ * (qc - 1))
                        sync.dma_start(qaug[0:64, qpar * 512:(qpar + 1) * 512],
                                       AP(q1tp_h, h * 64 * S + qc * 512, [[S, 64], [1, 512]])
                                       ).then_inc(sQ, 16)
                        sync.dma_start(qaug[64:65, qpar * 512:(qpar + 1) * 512],
                                       AP(mu1_h, h * S + qc * 512, [[1, 1], [1, 512]])
                                       ).then_inc(sQ, 16)
                        sync.wait_ge(cV, V0 + 3 * qc + 3)
                        sync.dma_start(AP(attnt1_h, h * 64 * S + qc * 512,
                                          [[S, 64], [1, 512]]),
                                       att[:, qpar * 512:(qpar + 1) * 512]).then_inc(sO, 16)

                @blk.tensor
                def _(tensor):
                  with tensor.register(uname("rw")) as w:
                    with fori(tensor, NQI) as iq:
                        for qp in QPARS:
                            rwait(tensor, w, sQ, iq, 32 * NQP, qp * 32 + Q0 + 48)
                            for kt in range(NKT):
                                bk = kt % 4
                                rwait(tensor, w, cA, iq, NQP * AQ, qp * AQ + A0 + kt // 2 - 1)
                                if kt == 0:
                                    rwait(tensor, w, cV, iq, 3 * NQP, qp * 3 + V0 - 3)
                                tensor.matmul(psc[:, bk * 512:(bk + 1) * 512],
                                              kbuf[0:65, par * S + kt * 128:par * S + (kt + 1) * 128],
                                              qaug[0:65, qp * 512:(qp + 1) * 512],
                                              start=True, stop=True).then_inc(cT, 1)
                                if kt >= 4:
                                    ka = kt - 4
                                    rwait(tensor, w, cA, iq, NQP * AQ, qp * AQ + A0 + ka // 2 + 1)
                                    tensor.matmul(pao[0:65, qp * 512:qp * 512 + 512],
                                                  vaug1[:, ka * 1040 + h * 65:
                                                        ka * 1040 + (h + 1) * 65],
                                                  exps[:, ((ka // 2) % 2) * 1024 + (ka % 2) * 512:
                                                       ((ka // 2) % 2) * 1024 + (ka % 2) * 512 + 512],
                                                  start=(ka == 0), stop=False).then_inc(cG, 1)
                            for ka in range(max(NKT - 4, 0), NKT):
                                rwait(tensor, w, cA, iq, NQP * AQ, qp * AQ + A0 + ka // 2 + 1)
                                tensor.matmul(pao[0:65, qp * 512:qp * 512 + 512],
                                              vaug1[:, ka * 1040 + h * 65:
                                                    ka * 1040 + (h + 1) * 65],
                                              exps[:, ((ka // 2) % 2) * 1024 + (ka % 2) * 512:
                                                   ((ka // 2) % 2) * 1024 + (ka % 2) * 512 + 512],
                                              start=(ka == 0), stop=(ka == NKT - 1)).then_inc(cG, 1)
                            rwait(tensor, w, cV, iq, 3 * NQP, qp * 3 + V0 + 1)
                            tensor.matmul(pao[0:64, 1024:1536], onescol[:], rcp[:],
                                          start=True, stop=True).then_inc(cT, 1)

                @blk.scalar
                def _(scalar):
                  with scalar.register(uname("rw")) as w:
                    with fori(scalar, NQI) as iq:
                        for qp in QPARS:
                            for kt in range(0, NKT, 2):
                                bk = kt % 4
                                rwait(scalar, w, cT, iq, NQP * TQ, qp * TQ + T0 + kt + 2)
                                rwait(scalar, w, cG, iq, NQP * NKT, qp * NKT + G0 + kt - 2)
                                scalar.activation(exps[:, ((kt // 2) % 2) * 1024:
                                                       ((kt // 2) % 2) * 1024 + 1024],
                                                  psc[:, bk * 512:bk * 512 + 1024],
                                                  AF.Exp, bias=0.0, scale=SCALE
                                                  ).then_inc(cA, 1)

                @blk.vector
                def _(vector):
                  with vector.register(uname("rw")) as w:
                    with fori(vector, NQI) as iq:
                        for qp in QPARS:
                            rwait(vector, w, cG, iq, NQP * NKT, qp * NKT + G0 + NKT)
                            with nc.allow_low_precision(reason="bf16 softmax recip"):
                                vector.reciprocal(rcp[:], pao[64:65, qp * 512:qp * 512 + 512]
                                                  ).then_inc(cV, 1)
                            rwait(vector, w, cT, iq, NQP * TQ, qp * TQ + T0 + TQ)
                            vector.tensor_copy(rb[:], pao[0:64, 1024:1536]).then_inc(cV, 1)
                            rwait(vector, w, sO, iq, 16 * NQP, qp * 16 + O0 - 16)
                            vector.tensor_tensor(att[:, qp * 512:(qp + 1) * 512],
                                                 pao[0:64, qp * 512:qp * 512 + 512], rb[:],
                                                 ALU.mult).then_inc(cV, 1)

                @blk.gpsimd
                def _(gpsimd):
                    for qc in range(0, NQC, 4):
                        gpsimd.wait_ge(cV, V0 + 3 * qc)
                        emit_casts(gpsimd, 6)

            L["sQ"] = Q0 + 16 * (1 + 2 * NQC)
            L["sO"] = O0 + 16 * NQC
            L["cT"] = T0 + TQ * NQC
            L["cG"] = G0 + NKT * NQC
            L["cA"] = A0 + AQ * NQC
            L["cV"] = V0 + 3 * NQC
            nc.all_engine_barrier()
        grp.close()

    pass2_unpacked()
    s1.close()

    def proj_phase(mode, attnt, w_t_h, bias_h, resid_h, o1_h, aw_h, ab_h, o2_h):
        grp = ExitStack()
        wsb = grp.enter_context(nc.sbuf_tensor(uname("wsb"), [128, 8 * 1024], BF16))
        abuf = grp.enter_context(nc.sbuf_tensor(uname("abuf"), [128, 2 * 1024], BF16))
        rbuf = grp.enter_context(nc.sbuf_tensor(uname("rbuf"), [128, 2 * 1024], F32))
        xa = grp.enter_context(nc.sbuf_tensor(uname("xa"), [128, 1024], F32))
        xo = grp.enter_context(nc.sbuf_tensor(uname("xo"), [128, 2 * 1024], F32))
        af = grp.enter_context(nc.sbuf_tensor(uname("af"), [128, 2 * 1024], BF16))
        sqdump = grp.enter_context(nc.sbuf_tensor(uname("sqdump"), [128, 1024], BF16))
        st1 = grp.enter_context(nc.sbuf_tensor(uname("st1"), [128, 8], F32))
        bb = grp.enter_context(nc.sbuf_tensor(uname("bb"), [128, 1024], F32)) if bias_h is not None else None
        aw = grp.enter_context(nc.sbuf_tensor(uname("aw"), [128, 1024], F32))
        ab = grp.enter_context(nc.sbuf_tensor(uname("ab"), [128, 1024], F32))
        psp = grp.enter_context(nc.psum_tensor(uname("psp"), [128, 2048], F32))
        if mode == "u":
            btsb = grp.enter_context(nc.sbuf_tensor(uname("btsb"), [128, 2 * 1024], BF16))
            gatesb = grp.enter_context(nc.sbuf_tensor(uname("gatesb"), [128, 64], BF16))
            gbb = grp.enter_context(nc.sbuf_tensor(uname("gbb"), [128, 8], F32))
            lg = grp.enter_context(nc.sbuf_tensor(uname("lg"), [128, 8], F32))
            mrow = grp.enter_context(nc.sbuf_tensor(uname("mrow"), [128, 8], F32))
            eq = grp.enter_context(nc.sbuf_tensor(uname("eq"), [128, 8], F32))
            cs = grp.enter_context(nc.sbuf_tensor(uname("cs"), [128, 8], F32))
            c1 = grp.enter_context(nc.sbuf_tensor(uname("c1"), [128, 8], F32))
            tpsb = grp.enter_context(nc.psum_tensor(uname("tpsb"), [128, 512], BF16))
            pg = grp.enter_context(nc.psum_tensor(uname("pg"), [128, 512], F32))
            mstage2 = grp.enter_context(nc.sbuf_tensor(uname("mst2"), [128, 2 * 8], F32))
        base = dict(L)
        nld = (1 if bias_h is not None else 0) + 2 + (1 if mode == "u" else 0)
        T0, A0, V0, G0, L0, O0 = (base["cT"], base["cA"], base["cV"], base["cG"],
                                  base["sL"], base["sO"])
        VPT = 10
        TPT = 1 if mode == "postln" else 4
        SOPT = 16 * (2 if mode == "postln" else 10)
        CGT = 2 if mode == "postln" else 33
        with nc.Block() as blk:
            @blk.gpsimd
            def _(gpsimd):
                for c in range(8):
                    gpsimd.dma_start(wsb[:, c * 1024:(c + 1) * 1024],
                                     AP(w_t_h, c * 128 * D, [[D, 128], [1, D]])
                                     ).then_inc(sWc, 16)
                L["sWc"] += 16 * 8
                if mode == "u":
                    for c in range(8):
                        gpsimd.dma_start(gatesb[:, c * 8:(c + 1) * 8],
                                         AP(gatew_h, c * 128 * NE, [[NE, 128], [1, NE]])
                                         ).then_inc(sWc, 16)
                    L["sWc"] += 16 * 8
                gpsimd.wait_ge(sWc, L["sWc"])
                for t in range(0, NT, 4):
                    gpsimd.wait_ge(cG, G0 + CGT * t)
                    emit_casts(gpsimd, 5)

            @blk.sync
            def _(sync):
                if bias_h is not None:
                    sync.dma_start(bb[:], AP(bias_h, 0, [[0, 128], [1, D]])).then_inc(sL, 16)
                sync.dma_start(aw[:], AP(aw_h, 0, [[0, 128], [1, D]])).then_inc(sL, 16)
                sync.dma_start(ab[:], AP(ab_h, 0, [[0, 128], [1, D]])).then_inc(sL, 16)
                if mode == "u":
                    sync.dma_start(gbb[:], AP(gateb_h, 0, [[0, 128], [1, NE]])).then_inc(sL, 16)
                for t in range(NT):
                    par = t % 2
                    if t >= 2:
                        sync.wait_ge(cV, V0 + VPT * (t - 1))
                    sync.dma_start(abuf[:, par * 1024:(par + 1) * 1024],
                                   AP(attnt, t * 128, [[S, 128], [128 * S, 8], [1, 128]])
                                   ).then_inc(sL, 16)
                    sync.dma_start(rbuf[:, par * 1024:(par + 1) * 1024],
                                   AP(resid_h, t * 128 * D, [[D, 128], [1, D]])
                                   ).then_inc(sL, 16)
                    sync.wait_ge(cV, V0 + VPT * t + 8)
                    sync.dma_start(AP(o1_h, t * 128 * D, [[D, 128], [1, D]]),
                                   xo[:, par * 1024:(par + 1) * 1024]).then_inc(sO, 16)
                    if dbg2_h is not None and t == 0 and not dbg2_used["v"]:
                        dbg2_used["v"] = True
                        sync.dma_start(AP(dbg2_h, 0, [[1032, 128], [1, 8]]),
                                       st1[:]).then_inc(sO, 16)
                        sync.dma_start(AP(dbg2_h, 8, [[1032, 128], [1, 1024]]),
                                       xa[:]).then_inc(sO, 16)
                        L["sO"] += 32
                    sync.wait_ge(cV, V0 + VPT * t + 10)
                    if mode == "postln":
                        sync.dma_start(AP(o2_h, t * 128 * D, [[D, 128], [1, D]]),
                                       af[:, par * 1024:(par + 1) * 1024]).then_inc(sO, 16)
                    else:
                        sync.dma_start(AP(bt_h, t * 128, [[S, 128], [128 * S, 8], [1, 128]]),
                                       btsb[:, par * 1024:(par + 1) * 1024]).then_inc(sO, 16)
                        for ee in range(NE):
                            sync.dma_start(AP(masks_h, ee * S + t * 128, [[1, 128], [1, 1]]),
                                           mstage2[:, par * 8 + ee:par * 8 + ee + 1]
                                           ).then_inc(sO, 16)

            @blk.tensor
            def _(tensor):
              with tensor.register(uname("rw")) as w:
                with fori(tensor, NT2) as it:
                    for par in range(2):
                        rwait(tensor, w, sL, it, 64, par * 32 + L0 + 16 * nld + 32)
                        rwait(tensor, w, cV, it, 2 * VPT, par * VPT + V0 - VPT + 1)
                        for n in range(2):
                            for c in range(8):
                                ins = tensor.matmul(
                                    psp[:, (par * 2 + n) * 512:(par * 2 + n + 1) * 512],
                                    abuf[:, par * 1024 + c * 128:par * 1024 + (c + 1) * 128],
                                    wsb[:, c * 1024 + n * 512:c * 1024 + (n + 1) * 512],
                                    start=(c == 0), stop=(c == 7))
                        ins.then_inc(cT, 1)
                        if mode == "u":
                            rwait(tensor, w, cV, it, 2 * VPT, par * VPT + V0 + 7)
                            for c in range(4):
                                ins = tensor.transpose(
                                    tpsb[:, c * 128:(c + 1) * 128],
                                    af[:, par * 1024 + c * 128:par * 1024 + (c + 1) * 128],
                                    ident[:])
                            ins.then_inc(cT, 1)
                            rwait(tensor, w, cV, it, 2 * VPT, par * VPT + V0 + 8)
                            for c in range(4):
                                ins = tensor.transpose(
                                    tpsb[:, c * 128:(c + 1) * 128],
                                    af[:, par * 1024 + (4 + c) * 128:par * 1024 + (5 + c) * 128],
                                    ident[:])
                            ins.then_inc(cT, 1)
                            rwait(tensor, w, cV, it, 2 * VPT, par * VPT + V0 + 9)
                            for c in range(8):
                                ins = tensor.matmul(
                                    pg[:, 0:NE],
                                    btsb[:, par * 1024 + c * 128:par * 1024 + (c + 1) * 128],
                                    gatesb[:, c * 8:c * 8 + NE],
                                    start=(c == 0), stop=(c == 7))
                            ins.then_inc(cT, 1)

            @blk.vector
            def _(vector):
              with vector.register(uname("rw")) as w, vector.register(uname("ro")) as o:
                with fori(vector, NT2) as it:
                    for par in range(2):
                        rwait(vector, w, cT, it, 2 * TPT, par * TPT + T0 + 1)
                        rwait(vector, w, sL, it, 64, par * 32 + L0 + 16 * nld + 32)
                        rwait(vector, w, sO, it, 2 * SOPT, par * SOPT + O0 - SOPT)
                        for n in range(2):
                            pslc = psp[:, (par * 2 + n) * 512:(par * 2 + n + 1) * 512]
                            if bias_h is not None:
                                vector.tensor_add(xa[:, n * 512:(n + 1) * 512], pslc,
                                                  bb[:, n * 512:(n + 1) * 512])
                                vector.tensor_add(
                                    xa[:, n * 512:(n + 1) * 512],
                                    xa[:, n * 512:(n + 1) * 512],
                                    rbuf[:, par * 1024 + n * 512:par * 1024 + (n + 1) * 512])
                            else:
                                vector.tensor_add(
                                    xa[:, n * 512:(n + 1) * 512], pslc,
                                    rbuf[:, par * 1024 + n * 512:par * 1024 + (n + 1) * 512])
                        vector.tensor_reduce(st1[:, 0:1], xa[:], AX.X, ALU.add,
                                             negate=True).then_inc(cV, 1)
                        rwait(vector, w, cV, it, 2 * VPT, par * VPT + V0 + 1)
                        vector.tensor_mul(st1[:, 1:2], st1[:, 0:1],
                                          invd[:]).then_inc(cG, 1)
                        rwait(vector, w, cG, it, 2 * CGT, par * CGT + G0 + 1)
                        vector.tensor_mul(st1[:, 2:3], st1[:, 1:2],
                                          st1[:, 1:2]).then_inc(cV, 1)
                        rwait(vector, w, cA, it, 4, par * 2 + A0 + 1)
                        vector.tensor_mul(st1[:, 0:1], st1[:, 3:4],
                                          invd[:]).then_inc(cG, 1)
                        rwait(vector, w, cG, it, 2 * CGT, par * CGT + G0 + 2)
                        rwait(vector, w, cV, it, 2 * VPT, par * VPT + V0 + 2)
                        vector.tensor_sub(st1[:, 4:5], st1[:, 0:1],
                                          st1[:, 2:3]).then_inc(cV, 1)
                        rwait(vector, w, cA, it, 4, par * 2 + A0 + 2)
                        vector.reciprocal(st1[:, 6:7], st1[:, 5:6]).then_inc(cV, 1)
                        rwait(vector, w, cV, it, 2 * VPT, par * VPT + V0 + 4)
                        if mode == "postln":
                            vector.tensor_scalar(xo[:, par * 1024:(par + 1) * 1024],
                                                 xa[:], st1[:, 1:2], st1[:, 6:7],
                                                 ALU.add, ALU.mult).then_inc(cV, 1)
                        else:
                            vector.tensor_copy(xo[:, par * 1024:(par + 1) * 1024], xa[:])
                            vector.tensor_scalar(xa[:], xa[:], st1[:, 1:2], st1[:, 6:7],
                                                 ALU.add, ALU.mult).then_inc(cV, 1)
                        rwait(vector, w, cV, it, 2 * VPT, par * VPT + V0 + 5)
                        src = xo[:, par * 1024:(par + 1) * 1024] if mode == "postln" else xa[:]
                        vector.tensor_mul(af[:, par * 1024:(par + 1) * 1024], src, aw[:])
                        vector.tensor_add(af[:, par * 1024:(par + 1) * 1024],
                                          af[:, par * 1024:(par + 1) * 1024], ab[:])
                        vector.memset(st1[:, 7:8], 0.0).then_inc(cV, 1)
                        vector.memset(st1[:, 7:8], 0.0).then_inc(cV, 1)
                        if mode == "u":
                            rwait(vector, w, cT, it, 2 * TPT, par * TPT + T0 + 2)
                            vector.tensor_copy(btsb[:, par * 1024:par * 1024 + 512],
                                               tpsb[:, 0:512])
                            vector.memset(st1[:, 7:8], 0.0).then_inc(cV, 1)
                            rwait(vector, w, cT, it, 2 * TPT, par * TPT + T0 + 3)
                            vector.tensor_copy(btsb[:, par * 1024 + 512:par * 1024 + 1024],
                                               tpsb[:, 0:512])
                            vector.memset(st1[:, 7:8], 0.0).then_inc(cV, 1)
                            rwait(vector, w, cT, it, 2 * TPT, par * TPT + T0 + 4)
                            vector.tensor_add(lg[:], pg[:, 0:NE], gbb[:]).then_inc(cG, 1)
                            rwait(vector, w, cG, it, 2 * CGT, par * CGT + G0 + 3)
                            vector.reduce_max(mrow[:, 0:1], lg[:], AX.X).then_inc(cG, 2)
                            rwait(vector, w, cG, it, 2 * CGT, par * CGT + G0 + 5)
                            vector.tensor_single_scalar(eq[:], lg[:], mrow[:, 0:1],
                                                        ALU.is_equal).then_inc(cG, 4)
                            rwait(vector, w, cG, it, 2 * CGT, par * CGT + G0 + 9)
                            vector.tensor_tensor_scan(cs[:], eq[:], zeros[:, 0:8], 0.0,
                                                      ALU.add, ALU.add).then_inc(cG, 8)
                            rwait(vector, w, cG, it, 2 * CGT, par * CGT + G0 + 17)
                            vector.tensor_single_scalar(c1[:], cs[:], 1.0,
                                                        ALU.is_equal).then_inc(cG, 16)
                            rwait(vector, w, cG, it, 2 * CGT, par * CGT + G0 + 33)
                            vector.tensor_mul(mstage2[:, par * 8:(par + 1) * 8],
                                              eq[:], c1[:])
                            vector.memset(st1[:, 7:8], 0.0).then_inc(cV, 1)
                        else:
                            vector.memset(st1[:, 7:8], 0.0).then_inc(cV, 1)
                            vector.memset(st1[:, 7:8], 0.0).then_inc(cV, 1)
                            vector.memset(st1[:, 7:8], 0.0).then_inc(cV, 1)

            @blk.scalar
            def _(scalar):
              with scalar.register(uname("rw")) as w:
                with fori(scalar, NT2) as it:
                    for par in range(2):
                        rwait(scalar, w, cV, it, 2 * VPT, par * VPT + V0 + 1)
                        scalar.activation(sqdump[:], xa[:], AF.Square, bias=0.0,
                                          scale=1.0, accum_out=st1[:, 3:4]).then_inc(cA, 1)
                        rwait(scalar, w, cV, it, 2 * VPT, par * VPT + V0 + 3)
                        scalar.activation(st1[:, 5:6], st1[:, 4:5], AF.Sqrt, bias=epsc[:, 0:1],
                                          scale=1.0).then_inc(cA, 1)

        L["sL"] = L0 + 16 * (nld + 2 * NT)
        L["sO"] = O0 + SOPT * NT
        L["cT"] = T0 + TPT * NT
        L["cV"] = V0 + VPT * NT
        L["cA"] = A0 + 2 * NT
        L["cG"] = G0 + CGT * NT
        nc.all_engine_barrier()
        grp.close()

    proj_phase("postln", attnt1_h, woutt_h, bout_h, x_h, x1_h, ln1w_h, ln1b_h, ln1x_h)

    s2 = ExitStack()
    kts2 = [s2.enter_context(nc.sbuf_tensor(f"k2t{f}", [128, S], BF16)) for f in range(8)]
    vaug2 = s2.enter_context(nc.sbuf_tensor(uname("vaug2"), [128, NT * 1040], BF16))
    n2s2 = s2.enter_context(nc.sbuf_tensor(uname("n2s2"), [128, 32], F32))

    with nc.Block() as blk:
        @blk.vector
        def _(vector):
            vector.memset(AP(vaug2, 64, [[NT * 1040, 128], [65, 16 * NT], [1, 1]]), 1.0)
            vector.memset(n2s2[:], 0.0)
    nc.all_engine_barrier()

    wkv2_grp = ExitStack()
    wkv2 = wkv2_grp.enter_context(nc.sbuf_tensor(uname("wkv2"), [128, 8 * 2048], BF16))
    load_w_chunks(wkv2, wq2_h, D, 2048)
    kvq_phase("kv", False, ln1x_h, False, wkv2, 0, 2048, kts=kts2, vaugT=vaug2,
              kaug=None, n2_sb=n2s2, n2col=16)
    wkv2_grp.close()
    wq2_grp = ExitStack()
    wqq2 = wq2_grp.enter_context(nc.sbuf_tensor(uname("wqq2"), [128, 8 * 1024], BF16))
    load_w_chunks(wqq2, wq2_h, 0, 1024)
    kvq_phase("q", False, ln1x_h, False, wqq2, 0, 1024, qtp=q2tp_h, n2_sb=n2s2, n2col=0)
    wq2_grp.close()
    norm_finalize(n2s2, 1)

    def pass2_packed():
        grp = ExitStack()
        qb2 = grp.enter_context(nc.sbuf_tensor(uname("qb2"), [128, 2 * 512], BF16))
        exps = grp.enter_context(nc.sbuf_tensor(uname("exps2"), [128, 4 * 1024], BF16))
        rcp = grp.enter_context(nc.sbuf_tensor(uname("rcp2"), [1, 512], BF16))
        rb = grp.enter_context(nc.sbuf_tensor(uname("rb2"), [64, 512], BF16))
        att = grp.enter_context(nc.sbuf_tensor(uname("att2"), [64, 2 * 1024], BF16))
        psc = grp.enter_context(nc.psum_tensor(uname("psc2"), [128, 2048], F32))
        pao = grp.enter_context(nc.psum_tensor(uname("pao2"), [128, 1536], F32))
        NKT = NT
        TQ = NKT + 2
        QPARS = [0, 1] if NQC >= 2 else [0]
        NQP = len(QPARS)
        NQI = NQC // NQP
        for f in range(8):
            base = dict(L)
            T0, A0, V0, G0, Q0, O0 = (base["cT"], base["cA"], base["cV"],
                                      base["cG"], base["sQ"], base["sO"])
            with nc.Block() as blk:
                @blk.sync
                def _(sync):
                    for qc in range(NQC):
                        qpar = qc % 2
                        if qc >= 2:
                            sync.wait_ge(cT, T0 + TQ * (qc - 1))
                        sync.dma_start(qb2[:, qpar * 512:(qpar + 1) * 512],
                                       AP(q2tp_h, f * 128 * S + qc * 512, [[S, 128], [1, 512]])
                                       ).then_inc(sQ, 16)
                        sync.wait_ge(cV, V0 + 6 * qc + 3)
                        sync.dma_start(AP(attnt2_h, (2 * f) * 64 * S + qc * 512,
                                          [[S, 64], [1, 512]]),
                                       att[:, qpar * 1024:qpar * 1024 + 512]).then_inc(sO, 16)
                        sync.wait_ge(cV, V0 + 6 * qc + 6)
                        sync.dma_start(AP(attnt2_h, (2 * f + 1) * 64 * S + qc * 512,
                                          [[S, 64], [1, 512]]),
                                       att[:, qpar * 1024 + 512:qpar * 1024 + 1024]
                                       ).then_inc(sO, 16)

                @blk.tensor
                def _(tensor):
                  with tensor.register(uname("rw")) as w:
                    with fori(tensor, NQI) as iq:
                        for qp in QPARS:
                            rwait(tensor, w, sQ, iq, 16 * NQP, qp * 16 + Q0 + 16)
                            for kt in range(NKT):
                                bk = kt % 2
                                rwait(tensor, w, cA, iq, NQP * NKT, qp * NKT + A0 + kt - 1)
                                if kt == 0:
                                    rwait(tensor, w, cV, iq, 6 * NQP, qp * 6 + V0)
                                tensor.matmul(psc[:, bk * 1024:bk * 1024 + 512],
                                              kts2[f][0:64, kt * 128:(kt + 1) * 128],
                                              qb2[0:64, qp * 512:(qp + 1) * 512],
                                              start=True, stop=True)
                                tensor.matmul(psc[:, bk * 1024 + 512:bk * 1024 + 1024],
                                              kts2[f][64:128, kt * 128:(kt + 1) * 128],
                                              qb2[64:128, qp * 512:(qp + 1) * 512],
                                              start=True, stop=True,
                                              tile_position=(64, 0)).then_inc(cT, 1)
                                if kt >= 2:
                                    ka = kt - 2
                                    rwait(tensor, w, cA, iq, NQP * NKT, qp * NKT + A0 + ka + 1)
                                    tensor.matmul(pao[0:65, 0:512],
                                                  vaug2[:, ka * 1040 + (2 * f) * 65:
                                                        ka * 1040 + (2 * f) * 65 + 65],
                                                  exps[:, (ka % 4) * 1024:(ka % 4) * 1024 + 512],
                                                  start=(ka == 0), stop=False)
                                    tensor.matmul(pao[0:65, 512:1024],
                                                  vaug2[:, ka * 1040 + (2 * f + 1) * 65:
                                                        ka * 1040 + (2 * f + 1) * 65 + 65],
                                                  exps[:, (ka % 4) * 1024 + 512:(ka % 4) * 1024 + 1024],
                                                  start=(ka == 0), stop=False).then_inc(cG, 2)
                            for ka in (NKT - 2, NKT - 1):
                                rwait(tensor, w, cA, iq, NQP * NKT, qp * NKT + A0 + ka + 1)
                                tensor.matmul(pao[0:65, 0:512],
                                              vaug2[:, ka * 1040 + (2 * f) * 65:
                                                    ka * 1040 + (2 * f) * 65 + 65],
                                              exps[:, (ka % 4) * 1024:(ka % 4) * 1024 + 512],
                                              start=False, stop=(ka == NKT - 1))
                                tensor.matmul(pao[0:65, 512:1024],
                                              vaug2[:, ka * 1040 + (2 * f + 1) * 65:
                                                    ka * 1040 + (2 * f + 1) * 65 + 65],
                                              exps[:, (ka % 4) * 1024 + 512:(ka % 4) * 1024 + 1024],
                                              start=False, stop=(ka == NKT - 1)).then_inc(cG, 2)
                            rwait(tensor, w, cV, iq, 6 * NQP, qp * 6 + V0 + 1)
                            tensor.matmul(pao[0:64, 1024:1536], onescol[:], rcp[:],
                                          start=True, stop=True).then_inc(cT, 1)
                            rwait(tensor, w, cV, iq, 6 * NQP, qp * 6 + V0 + 4)
                            tensor.matmul(pao[0:64, 1024:1536], onescol[:], rcp[:],
                                          start=True, stop=True).then_inc(cT, 1)

                @blk.scalar
                def _(scalar):
                  with scalar.register(uname("rw")) as w:
                    with fori(scalar, NQI) as iq:
                        for qp in QPARS:
                            for kt in range(NKT):
                                bk = kt % 2
                                rwait(scalar, w, cT, iq, NQP * TQ, qp * TQ + T0 + kt + 1)
                                rwait(scalar, w, cG, iq, NQP * 2 * NKT, qp * 2 * NKT + G0 + 2 * kt - 6)
                                scalar.activation(exps[:, (kt % 4) * 1024:(kt % 4) * 1024 + 512],
                                                  psc[:, bk * 1024:bk * 1024 + 512],
                                                  AF.Exp, bias=mb2[:, 2 * f:2 * f + 1],
                                                  scale=SCALE)
                                scalar.activation(exps[:, (kt % 4) * 1024 + 512:
                                                       (kt % 4) * 1024 + 1024],
                                                  psc[:, bk * 1024 + 512:bk * 1024 + 1024],
                                                  AF.Exp, bias=mb2[:, 2 * f + 1:2 * f + 2],
                                                  scale=SCALE).then_inc(cA, 1)

                @blk.vector
                def _(vector):
                  with vector.register(uname("rw")) as w:
                    with fori(vector, NQI) as iq:
                        for qp in QPARS:
                            rwait(vector, w, cG, iq, NQP * 2 * NKT, qp * 2 * NKT + G0 + 2 * NKT)
                            with nc.allow_low_precision(reason="bf16 softmax recip"):
                                vector.reciprocal(rcp[:], pao[64:65, 0:512]).then_inc(cV, 1)
                            rwait(vector, w, cT, iq, NQP * TQ, qp * TQ + T0 + NKT + 1)
                            vector.tensor_copy(rb[:], pao[0:64, 1024:1536]).then_inc(cV, 1)
                            rwait(vector, w, sO, iq, 32 * NQP, qp * 32 + O0 - 32)
                            vector.tensor_tensor(att[:, qp * 1024:qp * 1024 + 512],
                                                 pao[0:64, 0:512], rb[:],
                                                 ALU.mult).then_inc(cV, 1)
                            with nc.allow_low_precision(reason="bf16 softmax recip"):
                                vector.reciprocal(rcp[:], pao[64:65, 512:1024]).then_inc(cV, 1)
                            rwait(vector, w, cT, iq, NQP * TQ, qp * TQ + T0 + NKT + 2)
                            vector.tensor_copy(rb[:], pao[0:64, 1024:1536]).then_inc(cV, 1)
                            vector.tensor_tensor(att[:, qp * 1024 + 512:qp * 1024 + 1024],
                                                 pao[0:64, 512:1024], rb[:],
                                                 ALU.mult).then_inc(cV, 1)

                @blk.gpsimd
                def _(gpsimd):
                    for qc in range(0, NQC, 2):
                        gpsimd.wait_ge(cV, V0 + 6 * qc)
                        emit_casts(gpsimd, 3)

            L["sQ"] = Q0 + 16 * NQC
            L["sO"] = O0 + 32 * NQC
            L["cT"] = T0 + TQ * NQC
            L["cG"] = G0 + 2 * NKT * NQC
            L["cA"] = A0 + NKT * NQC
            L["cV"] = V0 + 6 * NQC
            nc.all_engine_barrier()
        grp.close()

    pass2_packed()
    s2.close()

    proj_phase("u", attnt2_h, wproj_h, None, x1_h, u_h, ln2w_h, ln2b_h, None)

    moe = ExitStack()
    w1m = moe.enter_context(nc.sbuf_tensor(uname("w1m"), [128, 8 * HID], BF16))
    w2m = moe.enter_context(nc.sbuf_tensor(uname("w2m"), [128, 2 * 8192], BF16))
    btc = moe.enter_context(nc.sbuf_tensor(uname("btc"), [128, 2 * 4096], BF16))
    gsb = moe.enter_context(nc.sbuf_tensor(uname("gsb"), [128, 32 * 512], BF16))
    usb = moe.enter_context(nc.sbuf_tensor(uname("usb"), [128, 2 * 4096], F32))
    mstage = moe.enter_context(nc.sbuf_tensor(uname("mstg"), [128, 2 * 4], F32))
    psm = moe.enter_context(nc.psum_tensor(uname("psm"), [128, 4096], F32))
    NC = NQC
    PARS = [0, 1] if NC >= 2 else [0]
    NPAR = len(PARS)
    NC2m = NC // NPAR
    moe_sO0 = L["sO"]
    for e in range(NE):
        base = dict(L)
        T0, A0, V0, G0, Q0, O0 = (base["cT"], base["cA"], base["cV"], base["cG"],
                                  base["sQ"], base["sO"])
        with nc.Block() as blk:
            @blk.sync
            def _(sync):
                sync.wait_ge(sOc, 16 * (e + 1))
                if e > 0:
                    sync.wait_ge(cT, T0)
                for dc in range(8):
                    sync.dma_start(w1m[:, dc * HID:(dc + 1) * HID],
                                   AP(w1b_h, (e * D + dc * 128) * HID,
                                      [[HID, 128], [1, HID]])).then_inc(sQ, 16)
                for c in range(NC):
                    par = c % 2
                    if c >= 2:
                        sync.wait_ge(cT, T0 + 32 * (c - 1))
                    sync.dma_start(btc[:, par * 4096:(par + 1) * 4096],
                                   AP(bt_h, c * 512, [[S, 128], [128 * S, 8], [1, 512]])
                                   ).then_inc(sQ, 16)
                    if e >= 1:
                        sync.wait_ge(sO, moe_sO0 + 16 * ((e - 1) * NC + c + 1))
                    sync.dma_start(usb[:, par * 4096:(par + 1) * 4096],
                                   AP(u_h, c * 512 * D, [[D, 128], [128 * D, 4], [1, D]])
                                   ).then_inc(sQ, 16)
                    for tt in range(4):
                        sync.dma_start(mstage[:, par * 4 + tt:par * 4 + tt + 1],
                                       AP(masks_h, e * S + c * 512 + tt * 128,
                                          [[1, 128], [1, 1]])).then_inc(sQ, 16)
                    for g4 in range(4):
                        if g4 >= 2:
                            sync.wait_ge(cG, G0 + 32 * c + 8 * g4 - 8)
                        elif c >= 1:
                            sync.wait_ge(cG, G0 + 32 * (c - 1) + 8 * (g4 + 2) - 8)
                        sync.dma_start(w2m[:, (g4 % 2) * 8192:(g4 % 2) * 8192 + 8192],
                                       AP(w2b_h, (e * HID + (g4 * 8) * 128) * D,
                                          [[D, 128], [128 * D, 8], [1, D]])
                                       ).then_inc(sQ, 16)
                    sync.wait_ge(cV, V0 + 5 * c + 5)
                    sync.dma_start(AP(u_h, c * 512 * D, [[D, 128], [128 * D, 4], [1, D]]),
                                   usb[:, par * 4096:(par + 1) * 4096]).then_inc(sO, 16)

            @blk.tensor
            def _(tensor):
              with tensor.register(uname("rw")) as w:
                with fori(tensor, NC2m) as ic:
                  for par in PARS:
                    rwait(tensor, w, sQ, ic, 160 * NPAR, par * 160 + Q0 + 128 + 16)
                    rwait(tensor, w, cV, ic, 5 * NPAR, par * 5 + V0)
                    for ht in range(32):
                        bk = ht % 4
                        rwait(tensor, w, cA, ic, 32 * NPAR, par * 32 + A0 + ht - 3)
                        for dc in range(8):
                            ins = tensor.matmul(
                                psm[:, bk * 512:(bk + 1) * 512],
                                w1m[:, dc * HID + ht * 128:dc * HID + (ht + 1) * 128],
                                btc[:, par * 4096 + dc * 512:par * 4096 + (dc + 1) * 512],
                                start=(dc == 0), stop=(dc == 7))
                        ins.then_inc(cT, 1)
                    rwait(tensor, w, cA, ic, 32 * NPAR, par * 32 + A0 + 32)
                    for ht in range(32):
                        g4 = ht // 8
                        rwait(tensor, w, sQ, ic, 160 * NPAR,
                              par * 160 + Q0 + 224 + 16 * (g4 + 1))
                        for tt in range(4):
                            for n in range(2):
                                ins = tensor.matmul(
                                    psm[:, (tt * 2 + n) * 512:(tt * 2 + n + 1) * 512],
                                    gsb[:, ht * 512 + tt * 128:ht * 512 + (tt + 1) * 128],
                                    w2m[:, (g4 % 2) * 8192 + (ht % 8) * 1024 + n * 512:
                                         (g4 % 2) * 8192 + (ht % 8) * 1024 + (n + 1) * 512],
                                    start=(ht == 0), stop=(ht == 31))
                        ins.then_inc(cG, 1)

            @blk.scalar
            def _(scalar):
              with scalar.register(uname("rw")) as w:
                with fori(scalar, NC2m) as ic:
                  for par in PARS:
                    for ht in range(32):
                        bk = ht % 4
                        rwait(scalar, w, cT, ic, 32 * NPAR, par * 32 + T0 + ht + 1)
                        rwait(scalar, w, cG, ic, 32 * NPAR, par * 32 + G0 + ht + 1 - 32)
                        scalar.activation(gsb[:, ht * 512:(ht + 1) * 512],
                                          psm[:, bk * 512:(bk + 1) * 512],
                                          AF.Gelu_apprx_tanh, bias=0.0, scale=1.0
                                          ).then_inc(cA, 1)

            @blk.vector
            def _(vector):
              with vector.register(uname("rw")) as w:
                with fori(vector, NC2m) as ic:
                  for par in PARS:
                    rwait(vector, w, cG, ic, 32 * NPAR, par * 32 + G0 + 32)
                    rwait(vector, w, sQ, ic, 160 * NPAR, par * 160 + Q0 + 224)
                    vector.memset(gsb[:, 0:1], 0.0).then_inc(cV, 1)
                    for tt in range(4):
                        uslc = usb[:, par * 4096 + tt * 1024:par * 4096 + (tt + 1) * 1024]
                        vector.scalar_tensor_tensor(
                            uslc, psm[:, tt * 1024:(tt + 1) * 1024],
                            mstage[:, par * 4 + tt:par * 4 + tt + 1],
                            uslc, ALU.mult, ALU.add).then_inc(cV, 1)

            @blk.gpsimd
            def _(gpsimd):
                if e == 0:
                    finish_casts(gpsimd)

        L["sQ"] = Q0 + 16 * (8 + NC * 10)
        L["sO"] = O0 + 16 * NC
        L["cT"] = T0 + 32 * NC
        L["cA"] = A0 + 32 * NC
        L["cG"] = G0 + 32 * NC
        L["cV"] = V0 + 5 * NC
        nc.all_engine_barrier()
    moe.close()

    fin = ExitStack()
    fx1 = fin.enter_context(nc.sbuf_tensor(uname("fx1"), [128, 2 * 1024], F32))
    fu = fin.enter_context(nc.sbuf_tensor(uname("fu"), [128, 2 * 1024], F32))
    fxa = fin.enter_context(nc.sbuf_tensor(uname("fxa"), [128, 1024], F32))
    fxo = fin.enter_context(nc.sbuf_tensor(uname("fxo"), [128, 2 * 1024], F32))
    fsq = fin.enter_context(nc.sbuf_tensor(uname("fsq"), [128, 1024], BF16))
    fst = fin.enter_context(nc.sbuf_tensor(uname("fst"), [128, 8], F32))
    base = dict(L)
    T0, A0, V0, L0, O0 = base["cT"], base["cA"], base["cV"], base["sL"], base["sO"]
    G0f = base["cG"]
    with nc.Block() as blk:
        @blk.sync
        def _(sync):
            for t in range(NT):
                par = t % 2
                if t >= 2:
                    sync.wait_ge(cV, V0 + 5 * (t - 1))
                sync.dma_start(fx1[:, par * 1024:(par + 1) * 1024],
                               AP(x1_h, t * 128 * D, [[D, 128], [1, D]])).then_inc(sL, 16)
                sync.dma_start(fu[:, par * 1024:(par + 1) * 1024],
                               AP(u_h, t * 128 * D, [[D, 128], [1, D]])).then_inc(sL, 16)
                sync.wait_ge(cV, V0 + 5 * t + 5)
                sync.dma_start(AP(y_h, t * 128 * D, [[D, 128], [1, D]]),
                               fxo[:, par * 1024:(par + 1) * 1024]).then_inc(sO, 16)
            sync.wait_ge(sO, O0 + 16 * NT)

        @blk.vector
        def _(vector):
          with vector.register(uname("rw")) as w:
            with fori(vector, NT2) as it:
                for par in range(2):
                    rwait(vector, w, sL, it, 64, par * 32 + L0 + 32)
                    vector.tensor_add(fxa[:], fx1[:, par * 1024:(par + 1) * 1024],
                                      fu[:, par * 1024:(par + 1) * 1024])
                    vector.tensor_reduce(fst[:, 0:1], fxa[:], AX.X, ALU.add,
                                         negate=True).then_inc(cV, 1)
                    rwait(vector, w, cV, it, 10, par * 5 + V0 + 1)
                    vector.tensor_mul(fst[:, 1:2], fst[:, 0:1],
                                      invd[:]).then_inc(cG, 1)
                    rwait(vector, w, cG, it, 4, par * 2 + G0f + 1)
                    vector.tensor_mul(fst[:, 2:3], fst[:, 1:2],
                                      fst[:, 1:2]).then_inc(cV, 1)
                    rwait(vector, w, cA, it, 4, par * 2 + A0 + 1)
                    vector.tensor_mul(fst[:, 0:1], fst[:, 3:4],
                                      invd[:]).then_inc(cG, 1)
                    rwait(vector, w, cG, it, 4, par * 2 + G0f + 2)
                    rwait(vector, w, cV, it, 10, par * 5 + V0 + 2)
                    vector.tensor_sub(fst[:, 4:5], fst[:, 0:1],
                                      fst[:, 2:3]).then_inc(cV, 1)
                    rwait(vector, w, cA, it, 4, par * 2 + A0 + 2)
                    vector.reciprocal(fst[:, 6:7], fst[:, 5:6]).then_inc(cV, 1)
                    rwait(vector, w, cV, it, 10, par * 5 + V0 + 4)
                    vector.tensor_scalar(fxo[:, par * 1024:(par + 1) * 1024],
                                         fxa[:], fst[:, 1:2], fst[:, 6:7],
                                         ALU.add, ALU.mult).then_inc(cV, 1)

        @blk.scalar
        def _(scalar):
          with scalar.register(uname("rw")) as w:
            with fori(scalar, NT2) as it:
                for par in range(2):
                    rwait(scalar, w, cV, it, 10, par * 5 + V0 + 1)
                    scalar.activation(fsq[:], fxa[:], AF.Square, bias=0.0, scale=1.0,
                                      accum_out=fst[:, 3:4]).then_inc(cA, 1)
                    rwait(scalar, w, cV, it, 10, par * 5 + V0 + 3)
                    scalar.activation(fst[:, 5:6], fst[:, 4:5], AF.Sqrt, bias=epsc[:, 0:1],
                                      scale=1.0).then_inc(cA, 1)
    L["sL"] = L0 + 32 * NT
    L["sO"] = O0 + 16 * NT
    L["cV"] = V0 + 5 * NT
    L["cA"] = A0 + 2 * NT
    L["cG"] = G0f + 2 * NT
    return nc


def _host_prep(inputs, S):
    i = inputs
    wq1t = np.ascontiguousarray(np.asarray(i["L0_attn_w_qkv"]).T)
    woutt = np.ascontiguousarray(np.asarray(i["L0_attn_w_out"]).T)
    inv_freq = (1.0 / np.float32(10000.0) **
                (np.arange(0, HD, 2, dtype=np.float32) / np.float32(HD))).astype(np.float32)
    t = np.arange(S, dtype=np.float32)
    freqs = t[:, None] * inv_freq[None, :]
    cosr = np.tile(np.cos(freqs).astype(np.float32), (1, 16)).astype(BF)
    sinr = np.tile(np.sin(freqs).astype(np.float32), (1, 16)).astype(BF)
    ident = np.eye(128, dtype=np.float32).astype(BF)
    com = {
        "wq1t": wq1t, "woutt": woutt,
        "bout": np.asarray(i["L0_attn_b_out"]).reshape(1, D),
        "ln1w": np.asarray(i["L0_ffn_ln1_w"]).reshape(1, D),
        "ln1b": np.asarray(i["L0_ffn_ln1_b"]).reshape(1, D),
        "ln2w": np.asarray(i["L0_ffn_ln2_w"]).reshape(1, D),
        "ln2b": np.asarray(i["L0_ffn_ln2_b"]).reshape(1, D),
        "wq2": np.asarray(i["L0_ffn_w_qkv"]),
        "wproj": np.asarray(i["L0_ffn_w_proj"]),
        "gatew": np.asarray(i["L0_ffn_gate_w"]),
        "gateb": np.asarray(i["L0_ffn_gate_b"]).reshape(1, NE),
        "w1": np.asarray(i["L0_ffn_w1"]),
        "w2": np.asarray(i["L0_ffn_w2"]),
        "cosr": cosr, "sinr": sinr, "identh": ident,
    }
    com = {k: np.ascontiguousarray(v) for k, v in com.items()}
    return com


def kernel(**inputs):
    x = np.asarray(inputs["L0_attn_x"])
    B, S, _ = x.shape
    com = _host_prep(inputs, S)
    nc = build_nc(S)
    in_maps = [dict(com, x=np.ascontiguousarray(x[c])) for c in range(B)]
    res = run_bass_kernel_spmd(nc, in_maps, core_ids=list(range(B)))
    return np.stack([r["y"] for r in res.results]).astype(np.float32)


# version 11
# speedup vs baseline: 1.0516x; 1.0516x over previous
import numpy as np
import ml_dtypes
from contextlib import ExitStack

import concourse.bass as bass
import concourse.mybir as mybir
from concourse.bass_utils import run_bass_kernel_spmd

F32 = mybir.dt.float32
BF16 = mybir.dt.bfloat16
AF = mybir.ActivationFunctionType
ALU = mybir.AluOpType
AX = mybir.AxisListType
BF = ml_dtypes.bfloat16
AP = bass.AP
ds = bass.ds

D = 1024
H = 16
HD = 64
SCALE = HD ** -0.5
LAM = 64.0
EPS = 1e-5
NE = 8
HID = 4096
CRED = 64
CREDD = 1 << 18


def build_nc(S=4096, dbg=False):
    NT = S // 128
    NT2 = NT // 2
    NQC = S // 512
    NQC2 = max(NQC // 2, 1)
    nc = bass.Bass("TRN2", target_bir_lowering=False)

    x_h = nc.dram_tensor("x", [S, D], F32, kind="ExternalInput")
    wq1t_h = nc.dram_tensor("wq1t", [D, 3 * D], F32, kind="ExternalInput")
    woutt_h = nc.dram_tensor("woutt", [D, D], F32, kind="ExternalInput")
    bout_h = nc.dram_tensor("bout", [1, D], F32, kind="ExternalInput")
    ln1w_h = nc.dram_tensor("ln1w", [1, D], F32, kind="ExternalInput")
    ln1b_h = nc.dram_tensor("ln1b", [1, D], F32, kind="ExternalInput")
    ln2w_h = nc.dram_tensor("ln2w", [1, D], F32, kind="ExternalInput")
    ln2b_h = nc.dram_tensor("ln2b", [1, D], F32, kind="ExternalInput")
    wq2_h = nc.dram_tensor("wq2", [D, 3 * D], F32, kind="ExternalInput")
    wproj_h = nc.dram_tensor("wproj", [D, D], F32, kind="ExternalInput")
    gatew_h = nc.dram_tensor("gatew", [D, NE], F32, kind="ExternalInput")
    gateb_h = nc.dram_tensor("gateb", [1, NE], F32, kind="ExternalInput")
    w1_h = nc.dram_tensor("w1", [NE, D, HID], F32, kind="ExternalInput")
    w2_h = nc.dram_tensor("w2", [NE, HID, D], F32, kind="ExternalInput")
    cosr_h = nc.dram_tensor("cosr", [S, 512], BF16, kind="ExternalInput")
    sinr_h = nc.dram_tensor("sinr", [S, 512], BF16, kind="ExternalInput")
    ident_h = nc.dram_tensor("identh", [128, 128], BF16, kind="ExternalInput")
    y_h = nc.dram_tensor("y", [S, D], F32, kind="ExternalOutput")

    kd = dict(kind="ExternalOutput") if dbg else {}
    q1tp_h = nc.dram_tensor("q1tp", [D, S], BF16, **kd)
    mu1_h = nc.dram_tensor("mu1", [H, S], BF16, **kd)
    kaug1_h = nc.dram_tensor("kaug1", [H, HD + 1, S], BF16, **kd)
    attnt1_h = nc.dram_tensor("attnt1", [D, S], BF16, **kd)
    x1_h = nc.dram_tensor("x1s", [S, D], F32, **kd)
    ln1x_h = nc.dram_tensor("ln1xs", [S, D], BF16, **kd)
    q2tp_h = nc.dram_tensor("q2tp", [D, S], BF16, **kd)
    attnt2_h = nc.dram_tensor("attnt2", [D, S], BF16, **kd)
    u_h = nc.dram_tensor("us", [S, D], F32, **kd)
    bt_h = nc.dram_tensor("bts", [D, S], BF16, **kd)
    w1b_h = nc.dram_tensor("w1b", [NE, D, HID], BF16)
    w2b_h = nc.dram_tensor("w2b", [NE, HID, D], BF16)
    mh_h = nc.dram_tensor("mhs", [2, 16], F32, **(dict(kind="ExternalOutput") if dbg else {}))
    masks_h = nc.dram_tensor("masksh", [NE, S], F32, **(dict(kind="ExternalOutput") if dbg else {}))
    dbg1_h = nc.dram_tensor("dbg1", [128, 192], F32, kind="ExternalOutput") if dbg else None
    dbg2_h = nc.dram_tensor("dbg2", [128, 1032], F32, kind="ExternalOutput") if dbg else None
    dbg2_used = {"v": False}

    ctx = ExitStack()

    ident = ctx.enter_context(nc.sbuf_tensor("ident", [128, 128], BF16))
    onescol = ctx.enter_context(nc.sbuf_tensor("onescol", [1, 64], BF16))
    neg1row = ctx.enter_context(nc.sbuf_tensor("neg1row", [1, 512], BF16))
    mb1 = ctx.enter_context(nc.sbuf_tensor("mb1", [128, 16], F32))
    mbr = ctx.enter_context(nc.sbuf_tensor("mbr", [128, 16], F32))
    mb2 = ctx.enter_context(nc.sbuf_tensor("mb2", [128, 16], F32))
    zeros = ctx.enter_context(nc.sbuf_tensor("zeros", [128, 16], F32))
    epsc = ctx.enter_context(nc.sbuf_tensor("epsc", [128, 1], F32))
    invd = ctx.enter_context(nc.sbuf_tensor("invd", [128, 1], F32))

    from concourse.bass import OrderedEngineSet
    from contextlib import contextmanager

    @contextmanager
    def fori(eng, n):
        engines = OrderedEngineSet([eng.engine])
        name = f"ff{nc.next_id()}"
        ls, le = f"{name}_l", f"{name}_e"
        regs = nc.alloc_registers(f"{name}_i", engines=engines)
        nc.regs_mov(regs, 0)
        nc.br(ls, engines=engines)
        with nc.body(ls, valid_engines=engines):
            yield list(regs)[0]
            nc.regs_alu(regs, regs, 1, op=ALU.add)
            nc.br_lt(regs, n, on_true=ls, on_false=le, engines=engines)
        nc.switch_bb(le)
        for r in regs:
            eng.free_register(r)

    def rcalc(eng, w, i, mult, const):
        eng.reg_alu(w, i, mult, ALU.mult)
        if const > 0:
            eng.reg_alu(w, w, const, ALU.add)
        elif const < 0:
            eng.reg_alu(w, w, -const, ALU.subtract)
        return w

    def rwait(eng, w, semh, i, mult, const):
        eng.wait_ge(semh, rcalc(eng, w, i, mult, const))

    def sem(name):
        return ctx.enter_context(nc.semaphore(name))
    sL = sem("sL")
    sQ = sem("sQ")
    sO = sem("sO")
    sWc = sem("sWc")
    sOc = sem("sOc")
    cT = sem("cT")
    cV = sem("cV")
    cA = sem("cA")
    cG = sem("cG")
    L = {"sL": CREDD, "sQ": CREDD, "sO": CREDD, "sWc": 0, "sOc": 0,
         "cT": CRED, "cV": CRED, "cA": CRED, "cG": CRED}

    CW = 1024
    wstg = ctx.enter_context(nc.sbuf_tensor("wstg", [128, 4 * CW], BF16))
    cpub = ctx.enter_context(nc.sbuf_tensor("cpub", [1, 1], F32))
    sCl = [sem(f"sCl{k}") for k in range(4)]
    sCs = [sem(f"sCs{k}") for k in range(4)]
    cast_jobs = []
    for e in range(NE):
        for dc in range(8):
            for quar in range(4):
                cast_jobs.append((w1_h, w1b_h, (e * D + dc * 128) * HID + quar * CW, HID))
        for hc in range(32):
            cast_jobs.append((w2_h, w2b_h, (e * HID + hc * 128) * D, CW))
    EJOBS = len(cast_jobs) // NE
    cst = {"i": 0}

    def _cast_store(gpsimd, j):
        src_t, dst_t, off, stride = cast_jobs[j]
        k = j % 4
        gpsimd.wait_ge(sCl[k], 16 * (j // 4 + 1))
        gpsimd.dma_start(AP(dst_t, off, [[stride, 128], [1, CW]]),
                         wstg[:, k * CW:(k + 1) * CW]).then_inc(sCs[k], 16)
        if j % EJOBS == EJOBS - 1:
            for kk in range(4):
                gpsimd.wait_ge(sCs[kk], 16 * ((j + 1) // 4))
            gpsimd.dma_start(cpub[0:1, 0:1], zeros[0:1, 0:1]).then_inc(sOc, 16)

    def emit_casts(gpsimd, n):
        for _ in range(n):
            i = cst["i"]
            if i >= len(cast_jobs):
                return
            cst["i"] += 1
            src_t, dst_t, off, stride = cast_jobs[i]
            k = i % 4
            if i >= 4:
                gpsimd.wait_ge(sCs[k], 16 * (i // 4))
            gpsimd.dma_start(wstg[:, k * CW:(k + 1) * CW],
                             AP(src_t, off, [[stride, 128], [1, CW]])).then_inc(sCl[k], 16)
            if i >= 2:
                _cast_store(gpsimd, i - 2)

    def finish_casts(gpsimd):
        emit_casts(gpsimd, len(cast_jobs))
        for j in range(max(len(cast_jobs) - 2, 0), len(cast_jobs)):
            _cast_store(gpsimd, j)
        for k in range(4):
            njk = (len(cast_jobs) - 1 - k) // 4 + 1
            gpsimd.wait_ge(sCs[k], 16 * njk)

    with nc.Block() as blk:
        @blk.sync
        def _(sync):
            sync.dma_start(ident[:], ident_h[:, :]).then_inc(sL, 16)
            sync.wait_ge(sL, 16)

        @blk.vector
        def _(vector):
            vector.memset(onescol[:], 1.0).then_inc(cV, CRED)
            vector.memset(neg1row[:], -1.0).then_inc(cT, CRED)
            vector.memset(zeros[:], 0.0).then_inc(cA, CRED)
            vector.memset(epsc[:], EPS).then_inc(cG, CRED)
            vector.memset(invd[:], 1.0 / D)
            vector.memset(zeros[:, 0:1], 0.0).then_inc(sL, CREDD)
            vector.memset(zeros[:, 0:1], 0.0).then_inc(sQ, CREDD)
            vector.memset(zeros[:, 0:1], 0.0).then_inc(sO, CREDD)
    L["sL"] += 16
    nc.all_engine_barrier()

    with nc.Block() as blk:
        @blk.sync
        def _(sync):
            for h in range(H):
                for j in range(S // 512):
                    sync.dma_start(AP(kaug1_h, (h * (HD + 1) + HD) * S + j * 512,
                                      [[1, 1], [1, 512]]),
                                   neg1row[:, :]).then_inc(sO, 16)
            L["sO"] += 16 * H * (S // 512)
            sync.wait_ge(sO, L["sO"])
    nc.all_engine_barrier()

    PH = {"n": 0}

    def uname(base):
        PH["n"] += 1
        return f"{base}{PH['n']}"

    def kvq_phase(mode, rope, xin_h, xin_f32, w_sb, wcol0, wrow, kts=None, vaugT=None,
                  kaug=None, qtp=None, n2_sb=None, n2col=0):
        grp = ExitStack()
        xin = grp.enter_context(nc.sbuf_tensor(uname("xin"), [128, 2 * 1024], F32)) if xin_f32 else None
        xbf = grp.enter_context(nc.sbuf_tensor(uname("xbf"), [128, 2 * 1024], BF16))
        xtt = grp.enter_context(nc.sbuf_tensor(uname("xtt"), [128, 2 * 1024], BF16))
        rot = grp.enter_context(nc.sbuf_tensor(uname("rot"), [128, 2 * 1024], BF16))
        rtmp = grp.enter_context(nc.sbuf_tensor(uname("rtmp"), [128, 1024], BF16))
        cosb = grp.enter_context(nc.sbuf_tensor(uname("cosb"), [128, 2 * 512], BF16)) if rope else None
        sinb = grp.enter_context(nc.sbuf_tensor(uname("sinb"), [128, 2 * 512], BF16)) if rope else None
        sqd = grp.enter_context(nc.sbuf_tensor(uname("sqd"), [128, 1024], BF16))
        hsum = grp.enter_context(nc.sbuf_tensor(uname("hsum"), [128, 16], F32))
        qtt = grp.enter_context(nc.sbuf_tensor(uname("qtt"), [128, 2 * 1024], BF16)) if mode == "q" else None
        ktstage = grp.enter_context(nc.sbuf_tensor(uname("ktst"), [128, 2 * 1024], BF16)) if mode == "kv" else None
        vstage = grp.enter_context(nc.sbuf_tensor(uname("vst"), [128, 2 * 1024], BF16)) if mode == "kv" else None
        nfeat = 2048 if mode == "kv" else 1024
        ps = grp.enter_context(nc.psum_tensor(uname("pskvq"), [128, 2048], F32))
        tps = grp.enter_context(nc.psum_tensor(uname("tpskvq"), [128, 2048], BF16))
        base = dict(L)
        VPT, TPT, APT = 5, 3, 1
        DPT = 16 * (3 if rope else 1)
        if mode == "kv":
            DOPT = 16 * (9 + (16 if kaug is not None else 0))
        else:
            DOPT = 16
        V0, T0, A0, L0 = base["cV"], base["cT"], base["cA"], base["sL"]
        O00 = base["sO"]

        with nc.Block() as blk:
            @blk.sync
            def _(sync):
                for t in range(NT):
                    par = t % 2
                    if t >= 2:
                        sync.wait_ge(cV, V0 + VPT * (t - 1))
                    dst = (xin if xin_f32 else xbf)[:, par * 1024:(par + 1) * 1024]
                    sync.dma_start(dst, AP(xin_h, t * 128 * D, [[D, 128], [1, D]])
                                   ).then_inc(sL, 16)
                    if rope:
                        sync.dma_start(cosb[:, par * 512:(par + 1) * 512],
                                       AP(cosr_h, t * 128 * 512, [[512, 128], [1, 512]])
                                       ).then_inc(sL, 16)
                        sync.dma_start(sinb[:, par * 512:(par + 1) * 512],
                                       AP(sinr_h, t * 128 * 512, [[512, 128], [1, 512]])
                                       ).then_inc(sL, 16)
                    sync.wait_ge(cV, V0 + VPT * t + 5)
                    if mode == "kv":
                        for c in range(8):
                            sync.dma_start(kts[c][:, t * 128:(t + 1) * 128],
                                           ktstage[:, par * 1024 + c * 128:
                                                   par * 1024 + (c + 1) * 128]
                                           ).then_inc(sO, 16)
                        sync.dma_start(AP(vaugT, t * 1040,
                                          [[NT * 1040, 128], [65, 16], [1, 64]]),
                                       vstage[:, par * 1024:(par + 1) * 1024]
                                       ).then_inc(sO, 16)
                        if kaug is not None:
                            for f in range(8):
                                sync.dma_start(AP(kaug, (2 * f) * (HD + 1) * S + t * 128,
                                                  [[S, 64], [1, 128]]),
                                               ktstage[0:64, par * 1024 + f * 128:
                                                       par * 1024 + (f + 1) * 128]
                                               ).then_inc(sO, 16)
                                sync.dma_start(AP(kaug, (2 * f + 1) * (HD + 1) * S + t * 128,
                                                  [[S, 64], [1, 128]]),
                                               ktstage[64:128, par * 1024 + f * 128:
                                                       par * 1024 + (f + 1) * 128]
                                               ).then_inc(sO, 16)
                    if mode == "q":
                        sync.dma_start(AP(qtp, t * 128, [[S, 128], [128 * S, 8], [1, 128]]),
                                       qtt[:, par * 1024:(par + 1) * 1024]).then_inc(sO, 16)

            @blk.vector
            def _(vector):
              with vector.register(uname("rw")) as w:
                with fori(vector, NT2) as it:
                    for par in range(2):
                        rwait(vector, w, sL, it, 2 * DPT, par * DPT + L0 + DPT)
                        rwait(vector, w, sO, it, 2 * DOPT, par * DOPT + O00 - DOPT)
                        if xin_f32:
                            vector.tensor_copy(xbf[:, par * 1024:(par + 1) * 1024],
                                               xin[:, par * 1024:(par + 1) * 1024]
                                               ).then_inc(cV, 1)
                        else:
                            vector.memset(hsum[:, 0:1], 0.0).then_inc(cV, 1)
                        rwait(vector, w, cT, it, 2 * TPT, par * TPT + T0 + 1)
                        vector.tensor_copy(xtt[:, par * 1024:(par + 1) * 1024],
                                           tps[:, par * 1024:(par + 1) * 1024]
                                           ).then_inc(cV, 1)
                        rwait(vector, w, cT, it, 2 * TPT, par * TPT + T0 + 2)
                        if rope:
                            x1v = AP(ps, 0, [[2048, 128], [64, 16], [1, 32]])
                            x2v = AP(ps, 32, [[2048, 128], [64, 16], [1, 32]])
                            cv_ = AP(cosb, par * 512, [[1024, 128], [32, 16], [1, 32]])
                            sv_ = AP(sinb, par * 512, [[1024, 128], [32, 16], [1, 32]])
                            t1 = AP(rtmp, 0, [[1024, 128], [64, 16], [1, 32]])
                            t2 = AP(rtmp, 32, [[1024, 128], [64, 16], [1, 32]])
                            r1 = AP(rot, par * 1024, [[2048, 128], [64, 16], [1, 32]])
                            r2 = AP(rot, par * 1024 + 32, [[2048, 128], [64, 16], [1, 32]])
                            vector.tensor_mul(t1, x1v, sv_)
                            vector.tensor_mul(t2, x2v, sv_)
                            vector.tensor_mul(r1, x1v, cv_)
                            vector.tensor_tensor(r1, r1, t2, ALU.subtract)
                            vector.tensor_mul(r2, x2v, cv_)
                            vector.tensor_tensor(r2, r2, t1, ALU.add)
                        else:
                            vector.tensor_copy(rot[:, par * 1024:(par + 1) * 1024],
                                               ps[:, 0:1024])
                        if mode == "kv":
                            vector.tensor_copy(vstage[:, par * 1024:(par + 1) * 1024],
                                               ps[:, 1024:2048])
                        vector.memset(hsum[:, 0:1], 0.0).then_inc(cV, 1)
                        rwait(vector, w, cA, it, 2 * APT, par * APT + A0 + 1)
                        vector.tensor_reduce(hsum[:, 0:16],
                                             AP(sqd, 0, [[1024, 128], [64, 16], [1, 64]]),
                                             AX.X, ALU.add).then_inc(cV, 1)
                        rwait(vector, w, cV, it, 2 * VPT, par * VPT + V0 + 4)
                        vector.tensor_tensor(n2_sb[:, n2col:n2col + 16],
                                             n2_sb[:, n2col:n2col + 16],
                                             hsum[:, 0:16], ALU.max)
                        rwait(vector, w, cT, it, 2 * TPT, par * TPT + T0 + 3)
                        if mode == "kv":
                            vector.tensor_copy(ktstage[:, par * 1024:(par + 1) * 1024],
                                               tps[:, par * 1024:(par + 1) * 1024]
                                               ).then_inc(cV, 1)
                        else:
                            vector.tensor_copy(qtt[:, par * 1024:(par + 1) * 1024],
                                               tps[:, par * 1024:(par + 1) * 1024]
                                               ).then_inc(cV, 1)

            @blk.scalar
            def _(scalar):
              with scalar.register(uname("rw")) as w:
                with fori(scalar, NT2) as it:
                    for par in range(2):
                        rwait(scalar, w, cV, it, 2 * VPT, par * VPT + V0 + 3)
                        scalar.activation(sqd[:], rot[:, par * 1024:(par + 1) * 1024],
                                          AF.Square, bias=0.0, scale=1.0).then_inc(cA, 1)

            @blk.tensor
            def _(tensor):
              with tensor.register(uname("rw")) as w:
                with fori(tensor, NT2) as it:
                    for par in range(2):
                        rwait(tensor, w, cV, it, 2 * VPT, par * VPT + V0 + 1)
                        for c in range(8):
                            ins = tensor.transpose(
                                tps[:, par * 1024 + c * 128:par * 1024 + (c + 1) * 128],
                                xbf[:, par * 1024 + c * 128:par * 1024 + (c + 1) * 128],
                                ident[:])
                        ins.then_inc(cT, 1)
                        rwait(tensor, w, cV, it, 2 * VPT, par * VPT + V0 + 2)
                        for n in range(nfeat // 512):
                            for c in range(8):
                                ins = tensor.matmul(
                                    ps[:, n * 512:(n + 1) * 512],
                                    xtt[:, par * 1024 + c * 128:par * 1024 + (c + 1) * 128],
                                    w_sb[:, c * wrow + wcol0 + n * 512:
                                         c * wrow + wcol0 + (n + 1) * 512],
                                    start=(c == 0), stop=(c == 7))
                        ins.then_inc(cT, 1)
                        rwait(tensor, w, cV, it, 2 * VPT, par * VPT + V0 + 3)
                        for c in range(8):
                            ins = tensor.transpose(
                                tps[:, par * 1024 + c * 128:par * 1024 + (c + 1) * 128],
                                rot[:, par * 1024 + c * 128:par * 1024 + (c + 1) * 128],
                                ident[:])
                        ins.then_inc(cT, 1)

            @blk.gpsimd
            def _(gpsimd):
                for t in range(0, NT, 2):
                    gpsimd.wait_ge(cV, V0 + VPT * t)
                    emit_casts(gpsimd, 4)

        L["sL"] = L0 + DPT * NT
        L["cV"] = V0 + VPT * NT
        L["cT"] = T0 + TPT * NT
        L["cA"] = A0 + APT * NT
        L["sO"] = O00 + DOPT * NT
        nc.all_engine_barrier()
        grp.close()

    def norm_finalize(n2_sb, slot):
        grp = ExitStack()
        nbf = grp.enter_context(nc.sbuf_tensor(uname("nbf"), [128, 32], BF16))
        ntr = grp.enter_context(nc.sbuf_tensor(uname("ntr"), [16, 256], F32))
        nred = grp.enter_context(nc.sbuf_tensor(uname("nred"), [16, 8], F32))
        tpsn = grp.enter_context(nc.psum_tensor(uname("tpsn"), [16, 256], BF16))
        base = dict(L)
        with nc.Block() as blk:
            @blk.vector
            def _(vector):
                vector.tensor_copy(nbf[:], n2_sb[:, 0:32]).then_inc(cV, 1)
                vector.wait_ge(cT, base["cT"] + 1)
                vector.tensor_copy(ntr[:, 0:128], tpsn[0:16, 0:128])
                vector.tensor_copy(ntr[:, 128:256], tpsn[0:16, 128:256]).then_inc(cV, 1)
                vector.reduce_max(nred[:, 0:1], ntr[:, 0:128], AX.X).then_inc(cV, 1)
                vector.reduce_max(nred[:, 1:2], ntr[:, 128:256], AX.X).then_inc(cV, 1)
                vector.wait_ge(cA, base["cA"] + 1)
                vector.tensor_mul(nred[:, 4:5], nred[:, 2:3], nred[:, 3:4]).then_inc(cV, 1)

            @blk.tensor
            def _(tensor):
                tensor.wait_ge(cV, base["cV"] + 1)
                tensor.transpose(tpsn[0:16, 0:128], nbf[:, 0:16], ident[:])
                tensor.transpose(tpsn[0:16, 128:256], nbf[:, 16:32], ident[:]).then_inc(cT, 1)

            @blk.scalar
            def _(scalar):
                scalar.wait_ge(cV, base["cV"] + 4)
                scalar.activation(nred[:, 2:3], nred[:, 0:1], AF.Sqrt, bias=0.0, scale=1.0)
                scalar.activation(nred[:, 3:4], nred[:, 1:2], AF.Sqrt, bias=0.0,
                                  scale=1.0).then_inc(cA, 1)

            @blk.sync
            def _(sync):
                sync.wait_ge(cV, base["cV"] + 5)
                sync.dma_start(AP(mh_h, slot * 16, [[1, 16]]), nred[0:16, 4:5]
                               ).then_inc(sO, 16)
                sync.wait_ge(sO, base["sO"] + 16)
                sync.dma_start(mbr[:] if slot == 0 else mb2[:],
                               AP(mh_h, slot * 16, [[0, 128], [1, 16]])).then_inc(sL, 16)
                sync.wait_ge(sL, base["sL"] + 16)
        L["cV"] = base["cV"] + 5
        L["cT"] = base["cT"] + 1
        L["cA"] = base["cA"] + 1
        L["sO"] = base["sO"] + 16
        L["sL"] = base["sL"] + 16
        nc.all_engine_barrier()
        with nc.Block() as blk:
            @blk.vector
            def _(vector):
                if slot == 0:
                    vector.tensor_scalar_mul(mb1[:], mbr[:], -1.0 / LAM)
                else:
                    vector.tensor_scalar_mul(mb2[:], mb2[:], -SCALE)
        nc.all_engine_barrier()
        grp.close()

    s1 = ExitStack()
    kts1 = [s1.enter_context(nc.sbuf_tensor(f"k1t{f}", [128, S], BF16)) for f in range(8)]
    vaug1 = s1.enter_context(nc.sbuf_tensor(uname("vaug1"), [128, NT * 1040], BF16))
    n2s1 = s1.enter_context(nc.sbuf_tensor(uname("n2s1"), [128, 32], F32))

    def load_w_chunks(w_sb, src_h, col0, width):
        with nc.Block() as blk:
            @blk.gpsimd
            def _(gpsimd):
                for c in range(8):
                    gpsimd.dma_start(w_sb[:, c * width:(c + 1) * width],
                                     AP(src_h, c * 128 * 3 * D + col0,
                                        [[3 * D, 128], [1, width]])).then_inc(sWc, 16)
                L["sWc"] += 16 * 8
                gpsimd.wait_ge(sWc, L["sWc"])
        nc.all_engine_barrier()

    with nc.Block() as blk:
        @blk.vector
        def _(vector):
            vector.memset(AP(vaug1, 64, [[NT * 1040, 128], [65, 16 * NT], [1, 1]]), 1.0)
            vector.memset(n2s1[:], 0.0)
    nc.all_engine_barrier()

    wkv_grp = ExitStack()
    wkv1 = wkv_grp.enter_context(nc.sbuf_tensor(uname("wkv1"), [128, 8 * 2048], BF16))
    load_w_chunks(wkv1, wq1t_h, D, 2048)
    kvq_phase("kv", True, x_h, True, wkv1, 0, 2048, kts=kts1, vaugT=vaug1,
              kaug=kaug1_h, n2_sb=n2s1, n2col=16)
    wkv_grp.close()
    wq_grp = ExitStack()
    wqq1 = wq_grp.enter_context(nc.sbuf_tensor(uname("wqq1"), [128, 8 * 1024], BF16))
    load_w_chunks(wqq1, wq1t_h, 0, 1024)
    kvq_phase("q", True, x_h, True, wqq1, 0, 1024, qtp=q1tp_h, n2_sb=n2s1, n2col=0)
    wq_grp.close()
    norm_finalize(n2s1, 0)

    p1 = ExitStack()
    qbuf = p1.enter_context(nc.sbuf_tensor(uname("qbuf"), [128, 2 * 1024], BF16))
    dump = p1.enter_context(nc.sbuf_tensor(uname("dump"), [128, 1024], BF16))
    dacc = p1.enter_context(nc.sbuf_tensor(uname("dacc"), [128, 16 * NQC], F32))
    dsum = p1.enter_context(nc.sbuf_tensor(uname("dsum"), [128, 16], F32))
    lnd = p1.enter_context(nc.sbuf_tensor(uname("lnd"), [128, 16], F32))
    mutbf = p1.enter_context(nc.sbuf_tensor(uname("mutbf"), [128, 2 * 16], BF16))
    mutr = p1.enter_context(nc.sbuf_tensor(uname("mutr"), [16, 2 * 128], BF16))
    tpsm = p1.enter_context(nc.psum_tensor(uname("tpsm"), [16, 128], BF16))
    psp1 = p1.enter_context(nc.psum_tensor(uname("psp1"), [128, 2048], F32))
    base = dict(L)
    MMT = 8 * NQC
    T0, A0, V0, G0, L0 = base["cT"], base["cA"], base["cV"], base["cG"], base["sL"]
    O0 = base["sO"]
    with nc.Block() as blk:
        @blk.sync
        def _(sync):
            for t in range(NT):
                par = t % 2
                if t >= 2:
                    sync.wait_ge(cT, T0 + (MMT + 1) * (t - 1))
                sync.dma_start(qbuf[:, par * 1024:(par + 1) * 1024],
                               AP(q1tp_h, t * 128, [[S, 128], [128 * S, 8], [1, 128]])
                               ).then_inc(sL, 16)
                sync.wait_ge(cV, V0 + 3 * t + 3)
                sync.dma_start(AP(mu1_h, t * 128, [[S, 16], [1, 128]]),
                               mutr[0:16, par * 128:(par + 1) * 128]).then_inc(sO, 16)
            if dbg1_h is not None:
                sync.dma_start(AP(dbg1_h, 0, [[192, 128], [1, 128]]), dacc[:]).then_inc(sO, 16)
                sync.dma_start(AP(dbg1_h, 128, [[192, 128], [1, 16]]), dsum[:]).then_inc(sO, 16)
                sync.dma_start(AP(dbg1_h, 144, [[192, 128], [1, 16]]), lnd[:]).then_inc(sO, 16)
                sync.dma_start(AP(dbg1_h, 160, [[192, 128], [1, 16]]), lnd[:]).then_inc(sO, 16)
                L["sO"] += 64

        @blk.tensor
        def _(tensor):
          with tensor.register(uname("rw")) as w:
            with fori(tensor, NT2) as it:
                for par in range(2):
                    rwait(tensor, w, sL, it, 32, par * 16 + L0 + 16)
                    for b in range(8):
                        for kc in range(NQC):
                            i = b * NQC + kc
                            bk = (i % 2) * 2
                            rwait(tensor, w, cA, it, 2 * MMT, par * MMT + A0 + i - 1)
                            tensor.matmul(psp1[:, bk * 512:(bk + 1) * 512],
                                          qbuf[0:64, par * 1024 + b * 128:par * 1024 + (b + 1) * 128],
                                          kts1[b][0:64, kc * 512:(kc + 1) * 512],
                                          start=True, stop=True)
                            tensor.matmul(psp1[:, (bk + 1) * 512:(bk + 2) * 512],
                                          qbuf[64:128, par * 1024 + b * 128:par * 1024 + (b + 1) * 128],
                                          kts1[b][64:128, kc * 512:(kc + 1) * 512],
                                          start=True, stop=True,
                                          tile_position=(64, 0)).then_inc(cT, 1)
                    rwait(tensor, w, cV, it, 6, par * 3 + V0 + 2)
                    tensor.transpose(tpsm[0:16, :], mutbf[:, par * 16:(par + 1) * 16],
                                     ident[:]).then_inc(cT, 1)

        @blk.scalar
        def _(scalar):
          with scalar.register(uname("rw")) as w:
            with fori(scalar, NT2) as it:
                for par in range(2):
                    rwait(scalar, w, cV, it, 6, par * 3 + V0 - 2)
                    scalar.memzero(dacc[:])
                    for b in range(8):
                        for kc in range(NQC):
                            i = b * NQC + kc
                            bk = (i % 2) * 2
                            rwait(scalar, w, cT, it, 2 * (MMT + 1), par * (MMT + 1) + T0 + i + 1)
                            scalar.activation(dump[:, 0:512], psp1[:, bk * 512:(bk + 1) * 512],
                                              AF.Exp, bias=mb1[:, 2 * b:2 * b + 1],
                                              scale=1.0 / LAM,
                                              accum_out=dacc[:, 2 * b * NQC + kc:
                                                             2 * b * NQC + kc + 1])
                            scalar.activation(dump[:, 0:512], psp1[:, (bk + 1) * 512:(bk + 2) * 512],
                                              AF.Exp, bias=mb1[:, 2 * b + 1:2 * b + 2],
                                              scale=1.0 / LAM,
                                              accum_out=dacc[:, (2 * b + 1) * NQC + kc:
                                                             (2 * b + 1) * NQC + kc + 1]
                                              ).then_inc(cA, 1)
                    rwait(scalar, w, cV, it, 6, par * 3 + V0 + 1)
                    scalar.activation(lnd[:], dsum[:], AF.Ln, bias=0.0, scale=1.0
                                      ).then_inc(cG, 1)

        @blk.vector
        def _(vector):
          with vector.register(uname("rw")) as w, vector.register(uname("ro")) as o:
            with fori(vector, NT2) as it:
                for par in range(2):
                    rwait(vector, w, cA, it, 2 * MMT, par * MMT + A0 + MMT)
                    for hh in range(16):
                        ins = vector.tensor_reduce(dsum[:, hh:hh + 1],
                                                   dacc[:, hh * NQC:(hh + 1) * NQC],
                                                   AX.X, ALU.add)
                    ins.then_inc(cV, 1)
                    rwait(vector, w, cG, it, 2, par + G0 + 1)
                    vector.scalar_tensor_tensor(mutbf[:, par * 16:(par + 1) * 16],
                                                lnd[:], LAM, mbr[:, 0:16],
                                                ALU.mult, ALU.add).then_inc(cV, 1)
                    rwait(vector, w, cT, it, 2 * (MMT + 1), par * (MMT + 1) + T0 + MMT + 1)
                    rwait(vector, w, sO, it, 32, par * 16 + O0 - 16)
                    vector.tensor_copy(mutr[0:16, par * 128:(par + 1) * 128],
                                       tpsm[0:16, :]).then_inc(cV, 1)

        @blk.gpsimd
        def _(gpsimd):
            for t in range(0, NT, 2):
                gpsimd.wait_ge(cV, V0 + 3 * t)
                emit_casts(gpsimd, 8)

    L["sL"] = L0 + 16 * NT
    L["sO"] = O0 + 16 * NT
    L["cT"] = T0 + (MMT + 1) * NT
    L["cA"] = A0 + MMT * NT
    L["cG"] = G0 + NT
    L["cV"] = V0 + 3 * NT
    nc.all_engine_barrier()

    p1.close()

    def pass2_unpacked():
        grp = ExitStack()
        kbuf = grp.enter_context(nc.sbuf_tensor(uname("kbuf"), [128, 2 * S], BF16))
        qaug = grp.enter_context(nc.sbuf_tensor(uname("qaug"), [128, 2 * 512], BF16))
        exps = grp.enter_context(nc.sbuf_tensor(uname("exps"), [128, 4 * 512], BF16))
        rcp = grp.enter_context(nc.sbuf_tensor(uname("rcp"), [1, 512], BF16))
        rb = grp.enter_context(nc.sbuf_tensor(uname("rb"), [64, 512], BF16))
        att = grp.enter_context(nc.sbuf_tensor(uname("att"), [64, 2 * 512], BF16))
        psc = grp.enter_context(nc.psum_tensor(uname("psc"), [128, 2048], F32))
        pao = grp.enter_context(nc.psum_tensor(uname("pao"), [128, 1536], F32))
        NKT = NT
        TQ = NKT + 1
        AQ = NKT // 2
        QPARS = [0, 1] if NQC >= 2 else [0]
        NQP = len(QPARS)
        NQI = NQC // NQP
        base0 = dict(L)
        for h in range(H):
            par = h % 2
            base = dict(L)
            T0, A0, V0, G0, Q0, O0 = (base["cT"], base["cA"], base["cV"],
                                      base["cG"], base["sQ"], base["sO"])
            with nc.Block() as blk:
                @blk.sync
                def _(sync):
                    if h >= 2:
                        sync.wait_ge(cT, base0["cT"] + TQ * NQC * (h - 1))
                    sync.dma_start(kbuf[0:65, par * S:par * S + S],
                                   AP(kaug1_h, h * (HD + 1) * S, [[S, 65], [1, S]])
                                   ).then_inc(sQ, 16)
                    for qc in range(NQC):
                        qpar = qc % 2
                        if qc >= 2:
                            sync.wait_ge(cT, T0 + TQ * (qc - 1))
                        sync.dma_start(qaug[0:64, qpar * 512:(qpar + 1) * 512],
                                       AP(q1tp_h, h * 64 * S + qc * 512, [[S, 64], [1, 512]])
                                       ).then_inc(sQ, 16)
                        sync.dma_start(qaug[64:65, qpar * 512:(qpar + 1) * 512],
                                       AP(mu1_h, h * S + qc * 512, [[1, 1], [1, 512]])
                                       ).then_inc(sQ, 16)
                        sync.wait_ge(cV, V0 + 3 * qc + 3)
                        sync.dma_start(AP(attnt1_h, h * 64 * S + qc * 512,
                                          [[S, 64], [1, 512]]),
                                       att[:, qpar * 512:(qpar + 1) * 512]).then_inc(sO, 16)

                @blk.tensor
                def _(tensor):
                  with tensor.register(uname("rw")) as w:
                    with fori(tensor, NQI) as iq:
                        for qp in QPARS:
                            rwait(tensor, w, sQ, iq, 32 * NQP, qp * 32 + Q0 + 48)
                            for kt in range(NKT):
                                bk = kt % 4
                                rwait(tensor, w, cA, iq, NQP * AQ, qp * AQ + A0 + kt // 2 - 1)
                                if kt == 0:
                                    rwait(tensor, w, cV, iq, 3 * NQP, qp * 3 + V0 - 3)
                                tensor.matmul(psc[:, bk * 512:(bk + 1) * 512],
                                              kbuf[0:65, par * S + kt * 128:par * S + (kt + 1) * 128],
                                              qaug[0:65, qp * 512:(qp + 1) * 512],
                                              start=True, stop=True).then_inc(cT, 1)
                                if kt >= 4:
                                    ka = kt - 4
                                    rwait(tensor, w, cA, iq, NQP * AQ, qp * AQ + A0 + ka // 2 + 1)
                                    tensor.matmul(pao[0:65, qp * 512:qp * 512 + 512],
                                                  vaug1[:, ka * 1040 + h * 65:
                                                        ka * 1040 + (h + 1) * 65],
                                                  exps[:, ((ka // 2) % 2) * 1024 + (ka % 2) * 512:
                                                       ((ka // 2) % 2) * 1024 + (ka % 2) * 512 + 512],
                                                  start=(ka == 0), stop=False).then_inc(cG, 1)
                            for ka in range(max(NKT - 4, 0), NKT):
                                rwait(tensor, w, cA, iq, NQP * AQ, qp * AQ + A0 + ka // 2 + 1)
                                tensor.matmul(pao[0:65, qp * 512:qp * 512 + 512],
                                              vaug1[:, ka * 1040 + h * 65:
                                                    ka * 1040 + (h + 1) * 65],
                                              exps[:, ((ka // 2) % 2) * 1024 + (ka % 2) * 512:
                                                   ((ka // 2) % 2) * 1024 + (ka % 2) * 512 + 512],
                                              start=(ka == 0), stop=(ka == NKT - 1)).then_inc(cG, 1)
                            rwait(tensor, w, cV, iq, 3 * NQP, qp * 3 + V0 + 1)
                            tensor.matmul(pao[0:64, 1024:1536], onescol[:], rcp[:],
                                          start=True, stop=True).then_inc(cT, 1)

                @blk.scalar
                def _(scalar):
                  with scalar.register(uname("rw")) as w:
                    with fori(scalar, NQI) as iq:
                        for qp in QPARS:
                            for kt in range(0, NKT, 2):
                                bk = kt % 4
                                rwait(scalar, w, cT, iq, NQP * TQ, qp * TQ + T0 + kt + 2)
                                rwait(scalar, w, cG, iq, NQP * NKT, qp * NKT + G0 + kt - 2)
                                scalar.activation(exps[:, ((kt // 2) % 2) * 1024:
                                                       ((kt // 2) % 2) * 1024 + 1024],
                                                  psc[:, bk * 512:bk * 512 + 1024],
                                                  AF.Exp, bias=0.0, scale=SCALE
                                                  ).then_inc(cA, 1)

                @blk.vector
                def _(vector):
                  with vector.register(uname("rw")) as w:
                    with fori(vector, NQI) as iq:
                        for qp in QPARS:
                            rwait(vector, w, cG, iq, NQP * NKT, qp * NKT + G0 + NKT)
                            with nc.allow_low_precision(reason="bf16 softmax recip"):
                                vector.reciprocal(rcp[:], pao[64:65, qp * 512:qp * 512 + 512]
                                                  ).then_inc(cV, 1)
                            rwait(vector, w, cT, iq, NQP * TQ, qp * TQ + T0 + TQ)
                            vector.tensor_copy(rb[:], pao[0:64, 1024:1536]).then_inc(cV, 1)
                            rwait(vector, w, sO, iq, 16 * NQP, qp * 16 + O0 - 16)
                            vector.tensor_tensor(att[:, qp * 512:(qp + 1) * 512],
                                                 pao[0:64, qp * 512:qp * 512 + 512], rb[:],
                                                 ALU.mult).then_inc(cV, 1)

                @blk.gpsimd
                def _(gpsimd):
                    for qc in range(0, NQC, 4):
                        gpsimd.wait_ge(cV, V0 + 3 * qc)
                        emit_casts(gpsimd, 4)

            L["sQ"] = Q0 + 16 * (1 + 2 * NQC)
            L["sO"] = O0 + 16 * NQC
            L["cT"] = T0 + TQ * NQC
            L["cG"] = G0 + NKT * NQC
            L["cA"] = A0 + AQ * NQC
            L["cV"] = V0 + 3 * NQC
            nc.all_engine_barrier()
        grp.close()

    pass2_unpacked()
    s1.close()

    def proj_phase(mode, attnt, w_t_h, bias_h, resid_h, o1_h, aw_h, ab_h, o2_h):
        grp = ExitStack()
        wsb = grp.enter_context(nc.sbuf_tensor(uname("wsb"), [128, 8 * 1024], BF16))
        abuf = grp.enter_context(nc.sbuf_tensor(uname("abuf"), [128, 2 * 1024], BF16))
        rbuf = grp.enter_context(nc.sbuf_tensor(uname("rbuf"), [128, 2 * 1024], F32))
        xa = grp.enter_context(nc.sbuf_tensor(uname("xa"), [128, 1024], F32))
        xo = grp.enter_context(nc.sbuf_tensor(uname("xo"), [128, 2 * 1024], F32))
        af = grp.enter_context(nc.sbuf_tensor(uname("af"), [128, 2 * 1024], BF16))
        sqdump = grp.enter_context(nc.sbuf_tensor(uname("sqdump"), [128, 1024], BF16))
        st1 = grp.enter_context(nc.sbuf_tensor(uname("st1"), [128, 8], F32))
        bb = grp.enter_context(nc.sbuf_tensor(uname("bb"), [128, 1024], F32)) if bias_h is not None else None
        aw = grp.enter_context(nc.sbuf_tensor(uname("aw"), [128, 1024], F32))
        ab = grp.enter_context(nc.sbuf_tensor(uname("ab"), [128, 1024], F32))
        psp = grp.enter_context(nc.psum_tensor(uname("psp"), [128, 2048], F32))
        if mode == "u":
            btsb = grp.enter_context(nc.sbuf_tensor(uname("btsb"), [128, 2 * 1024], BF16))
            gatesb = grp.enter_context(nc.sbuf_tensor(uname("gatesb"), [128, 64], BF16))
            gbb = grp.enter_context(nc.sbuf_tensor(uname("gbb"), [128, 8], F32))
            lg = grp.enter_context(nc.sbuf_tensor(uname("lg"), [128, 8], F32))
            mrow = grp.enter_context(nc.sbuf_tensor(uname("mrow"), [128, 8], F32))
            eq = grp.enter_context(nc.sbuf_tensor(uname("eq"), [128, 8], F32))
            cs = grp.enter_context(nc.sbuf_tensor(uname("cs"), [128, 8], F32))
            c1 = grp.enter_context(nc.sbuf_tensor(uname("c1"), [128, 8], F32))
            tpsb = grp.enter_context(nc.psum_tensor(uname("tpsb"), [128, 512], BF16))
            pg = grp.enter_context(nc.psum_tensor(uname("pg"), [128, 512], F32))
            mstage2 = grp.enter_context(nc.sbuf_tensor(uname("mst2"), [128, 2 * 8], F32))
        base = dict(L)
        nld = (1 if bias_h is not None else 0) + 2 + (1 if mode == "u" else 0)
        T0, A0, V0, G0, L0, O0 = (base["cT"], base["cA"], base["cV"], base["cG"],
                                  base["sL"], base["sO"])
        VPT = 10
        TPT = 1 if mode == "postln" else 4
        SOPT = 16 * (2 if mode == "postln" else 10)
        CGT = 2 if mode == "postln" else 33
        with nc.Block() as blk:
            @blk.gpsimd
            def _(gpsimd):
                for c in range(8):
                    gpsimd.dma_start(wsb[:, c * 1024:(c + 1) * 1024],
                                     AP(w_t_h, c * 128 * D, [[D, 128], [1, D]])
                                     ).then_inc(sWc, 16)
                L["sWc"] += 16 * 8
                if mode == "u":
                    for c in range(8):
                        gpsimd.dma_start(gatesb[:, c * 8:(c + 1) * 8],
                                         AP(gatew_h, c * 128 * NE, [[NE, 128], [1, NE]])
                                         ).then_inc(sWc, 16)
                    L["sWc"] += 16 * 8
                gpsimd.wait_ge(sWc, L["sWc"])
                for t in range(0, NT, 4):
                    gpsimd.wait_ge(cG, G0 + CGT * t)
                    emit_casts(gpsimd, 5)

            @blk.sync
            def _(sync):
                if bias_h is not None:
                    sync.dma_start(bb[:], AP(bias_h, 0, [[0, 128], [1, D]])).then_inc(sL, 16)
                sync.dma_start(aw[:], AP(aw_h, 0, [[0, 128], [1, D]])).then_inc(sL, 16)
                sync.dma_start(ab[:], AP(ab_h, 0, [[0, 128], [1, D]])).then_inc(sL, 16)
                if mode == "u":
                    sync.dma_start(gbb[:], AP(gateb_h, 0, [[0, 128], [1, NE]])).then_inc(sL, 16)
                for t in range(NT):
                    par = t % 2
                    if t >= 2:
                        sync.wait_ge(cV, V0 + VPT * (t - 1))
                    sync.dma_start(abuf[:, par * 1024:(par + 1) * 1024],
                                   AP(attnt, t * 128, [[S, 128], [128 * S, 8], [1, 128]])
                                   ).then_inc(sL, 16)
                    sync.dma_start(rbuf[:, par * 1024:(par + 1) * 1024],
                                   AP(resid_h, t * 128 * D, [[D, 128], [1, D]])
                                   ).then_inc(sL, 16)
                    sync.wait_ge(cV, V0 + VPT * t + 8)
                    sync.dma_start(AP(o1_h, t * 128 * D, [[D, 128], [1, D]]),
                                   xo[:, par * 1024:(par + 1) * 1024]).then_inc(sO, 16)
                    if dbg2_h is not None and t == 0 and not dbg2_used["v"]:
                        dbg2_used["v"] = True
                        sync.dma_start(AP(dbg2_h, 0, [[1032, 128], [1, 8]]),
                                       st1[:]).then_inc(sO, 16)
                        sync.dma_start(AP(dbg2_h, 8, [[1032, 128], [1, 1024]]),
                                       xa[:]).then_inc(sO, 16)
                        L["sO"] += 32
                    sync.wait_ge(cV, V0 + VPT * t + 10)
                    if mode == "postln":
                        sync.dma_start(AP(o2_h, t * 128 * D, [[D, 128], [1, D]]),
                                       af[:, par * 1024:(par + 1) * 1024]).then_inc(sO, 16)
                    else:
                        sync.dma_start(AP(bt_h, t * 128, [[S, 128], [128 * S, 8], [1, 128]]),
                                       btsb[:, par * 1024:(par + 1) * 1024]).then_inc(sO, 16)
                        for ee in range(NE):
                            sync.dma_start(AP(masks_h, ee * S + t * 128, [[1, 128], [1, 1]]),
                                           mstage2[:, par * 8 + ee:par * 8 + ee + 1]
                                           ).then_inc(sO, 16)

            @blk.tensor
            def _(tensor):
              with tensor.register(uname("rw")) as w:
                with fori(tensor, NT2) as it:
                    for par in range(2):
                        rwait(tensor, w, sL, it, 64, par * 32 + L0 + 16 * nld + 32)
                        rwait(tensor, w, cV, it, 2 * VPT, par * VPT + V0 - VPT + 1)
                        for n in range(2):
                            for c in range(8):
                                ins = tensor.matmul(
                                    psp[:, (par * 2 + n) * 512:(par * 2 + n + 1) * 512],
                                    abuf[:, par * 1024 + c * 128:par * 1024 + (c + 1) * 128],
                                    wsb[:, c * 1024 + n * 512:c * 1024 + (n + 1) * 512],
                                    start=(c == 0), stop=(c == 7))
                        ins.then_inc(cT, 1)
                        if mode == "u":
                            rwait(tensor, w, cV, it, 2 * VPT, par * VPT + V0 + 7)
                            for c in range(4):
                                ins = tensor.transpose(
                                    tpsb[:, c * 128:(c + 1) * 128],
                                    af[:, par * 1024 + c * 128:par * 1024 + (c + 1) * 128],
                                    ident[:])
                            ins.then_inc(cT, 1)
                            rwait(tensor, w, cV, it, 2 * VPT, par * VPT + V0 + 8)
                            for c in range(4):
                                ins = tensor.transpose(
                                    tpsb[:, c * 128:(c + 1) * 128],
                                    af[:, par * 1024 + (4 + c) * 128:par * 1024 + (5 + c) * 128],
                                    ident[:])
                            ins.then_inc(cT, 1)
                            rwait(tensor, w, cV, it, 2 * VPT, par * VPT + V0 + 9)
                            for c in range(8):
                                ins = tensor.matmul(
                                    pg[:, 0:NE],
                                    btsb[:, par * 1024 + c * 128:par * 1024 + (c + 1) * 128],
                                    gatesb[:, c * 8:c * 8 + NE],
                                    start=(c == 0), stop=(c == 7))
                            ins.then_inc(cT, 1)

            @blk.vector
            def _(vector):
              with vector.register(uname("rw")) as w, vector.register(uname("ro")) as o:
                with fori(vector, NT2) as it:
                    for par in range(2):
                        rwait(vector, w, cT, it, 2 * TPT, par * TPT + T0 + 1)
                        rwait(vector, w, sL, it, 64, par * 32 + L0 + 16 * nld + 32)
                        rwait(vector, w, sO, it, 2 * SOPT, par * SOPT + O0 - SOPT)
                        for n in range(2):
                            pslc = psp[:, (par * 2 + n) * 512:(par * 2 + n + 1) * 512]
                            if bias_h is not None:
                                vector.tensor_add(xa[:, n * 512:(n + 1) * 512], pslc,
                                                  bb[:, n * 512:(n + 1) * 512])
                                vector.tensor_add(
                                    xa[:, n * 512:(n + 1) * 512],
                                    xa[:, n * 512:(n + 1) * 512],
                                    rbuf[:, par * 1024 + n * 512:par * 1024 + (n + 1) * 512])
                            else:
                                vector.tensor_add(
                                    xa[:, n * 512:(n + 1) * 512], pslc,
                                    rbuf[:, par * 1024 + n * 512:par * 1024 + (n + 1) * 512])
                        vector.tensor_reduce(st1[:, 0:1], xa[:], AX.X, ALU.add,
                                             negate=True).then_inc(cV, 1)
                        rwait(vector, w, cV, it, 2 * VPT, par * VPT + V0 + 1)
                        vector.tensor_mul(st1[:, 1:2], st1[:, 0:1],
                                          invd[:]).then_inc(cG, 1)
                        rwait(vector, w, cG, it, 2 * CGT, par * CGT + G0 + 1)
                        vector.tensor_mul(st1[:, 2:3], st1[:, 1:2],
                                          st1[:, 1:2]).then_inc(cV, 1)
                        rwait(vector, w, cA, it, 4, par * 2 + A0 + 1)
                        vector.tensor_mul(st1[:, 0:1], st1[:, 3:4],
                                          invd[:]).then_inc(cG, 1)
                        rwait(vector, w, cG, it, 2 * CGT, par * CGT + G0 + 2)
                        rwait(vector, w, cV, it, 2 * VPT, par * VPT + V0 + 2)
                        vector.tensor_sub(st1[:, 4:5], st1[:, 0:1],
                                          st1[:, 2:3]).then_inc(cV, 1)
                        rwait(vector, w, cA, it, 4, par * 2 + A0 + 2)
                        vector.reciprocal(st1[:, 6:7], st1[:, 5:6]).then_inc(cV, 1)
                        rwait(vector, w, cV, it, 2 * VPT, par * VPT + V0 + 4)
                        if mode == "postln":
                            vector.tensor_scalar(xo[:, par * 1024:(par + 1) * 1024],
                                                 xa[:], st1[:, 1:2], st1[:, 6:7],
                                                 ALU.add, ALU.mult).then_inc(cV, 1)
                        else:
                            vector.tensor_copy(xo[:, par * 1024:(par + 1) * 1024], xa[:])
                            vector.tensor_scalar(xa[:], xa[:], st1[:, 1:2], st1[:, 6:7],
                                                 ALU.add, ALU.mult).then_inc(cV, 1)
                        rwait(vector, w, cV, it, 2 * VPT, par * VPT + V0 + 5)
                        src = xo[:, par * 1024:(par + 1) * 1024] if mode == "postln" else xa[:]
                        vector.tensor_mul(af[:, par * 1024:(par + 1) * 1024], src, aw[:])
                        vector.tensor_add(af[:, par * 1024:(par + 1) * 1024],
                                          af[:, par * 1024:(par + 1) * 1024], ab[:])
                        vector.memset(st1[:, 7:8], 0.0).then_inc(cV, 1)
                        vector.memset(st1[:, 7:8], 0.0).then_inc(cV, 1)
                        if mode == "u":
                            rwait(vector, w, cT, it, 2 * TPT, par * TPT + T0 + 2)
                            vector.tensor_copy(btsb[:, par * 1024:par * 1024 + 512],
                                               tpsb[:, 0:512])
                            vector.memset(st1[:, 7:8], 0.0).then_inc(cV, 1)
                            rwait(vector, w, cT, it, 2 * TPT, par * TPT + T0 + 3)
                            vector.tensor_copy(btsb[:, par * 1024 + 512:par * 1024 + 1024],
                                               tpsb[:, 0:512])
                            vector.memset(st1[:, 7:8], 0.0).then_inc(cV, 1)
                            rwait(vector, w, cT, it, 2 * TPT, par * TPT + T0 + 4)
                            vector.tensor_add(lg[:], pg[:, 0:NE], gbb[:]).then_inc(cG, 1)
                            rwait(vector, w, cG, it, 2 * CGT, par * CGT + G0 + 3)
                            vector.reduce_max(mrow[:, 0:1], lg[:], AX.X).then_inc(cG, 2)
                            rwait(vector, w, cG, it, 2 * CGT, par * CGT + G0 + 5)
                            vector.tensor_single_scalar(eq[:], lg[:], mrow[:, 0:1],
                                                        ALU.is_equal).then_inc(cG, 4)
                            rwait(vector, w, cG, it, 2 * CGT, par * CGT + G0 + 9)
                            vector.tensor_tensor_scan(cs[:], eq[:], zeros[:, 0:8], 0.0,
                                                      ALU.add, ALU.add).then_inc(cG, 8)
                            rwait(vector, w, cG, it, 2 * CGT, par * CGT + G0 + 17)
                            vector.tensor_single_scalar(c1[:], cs[:], 1.0,
                                                        ALU.is_equal).then_inc(cG, 16)
                            rwait(vector, w, cG, it, 2 * CGT, par * CGT + G0 + 33)
                            vector.tensor_mul(mstage2[:, par * 8:(par + 1) * 8],
                                              eq[:], c1[:])
                            vector.memset(st1[:, 7:8], 0.0).then_inc(cV, 1)
                        else:
                            vector.memset(st1[:, 7:8], 0.0).then_inc(cV, 1)
                            vector.memset(st1[:, 7:8], 0.0).then_inc(cV, 1)
                            vector.memset(st1[:, 7:8], 0.0).then_inc(cV, 1)

            @blk.scalar
            def _(scalar):
              with scalar.register(uname("rw")) as w:
                with fori(scalar, NT2) as it:
                    for par in range(2):
                        rwait(scalar, w, cV, it, 2 * VPT, par * VPT + V0 + 1)
                        scalar.activation(sqdump[:], xa[:], AF.Square, bias=0.0,
                                          scale=1.0, accum_out=st1[:, 3:4]).then_inc(cA, 1)
                        rwait(scalar, w, cV, it, 2 * VPT, par * VPT + V0 + 3)
                        scalar.activation(st1[:, 5:6], st1[:, 4:5], AF.Sqrt, bias=epsc[:, 0:1],
                                          scale=1.0).then_inc(cA, 1)

        L["sL"] = L0 + 16 * (nld + 2 * NT)
        L["sO"] = O0 + SOPT * NT
        L["cT"] = T0 + TPT * NT
        L["cV"] = V0 + VPT * NT
        L["cA"] = A0 + 2 * NT
        L["cG"] = G0 + CGT * NT
        nc.all_engine_barrier()
        grp.close()

    proj_phase("postln", attnt1_h, woutt_h, bout_h, x_h, x1_h, ln1w_h, ln1b_h, ln1x_h)

    s2 = ExitStack()
    kts2 = [s2.enter_context(nc.sbuf_tensor(f"k2t{f}", [128, S], BF16)) for f in range(8)]
    vaug2 = s2.enter_context(nc.sbuf_tensor(uname("vaug2"), [128, NT * 1040], BF16))
    n2s2 = s2.enter_context(nc.sbuf_tensor(uname("n2s2"), [128, 32], F32))

    with nc.Block() as blk:
        @blk.vector
        def _(vector):
            vector.memset(AP(vaug2, 64, [[NT * 1040, 128], [65, 16 * NT], [1, 1]]), 1.0)
            vector.memset(n2s2[:], 0.0)
    nc.all_engine_barrier()

    wkv2_grp = ExitStack()
    wkv2 = wkv2_grp.enter_context(nc.sbuf_tensor(uname("wkv2"), [128, 8 * 2048], BF16))
    load_w_chunks(wkv2, wq2_h, D, 2048)
    kvq_phase("kv", False, ln1x_h, False, wkv2, 0, 2048, kts=kts2, vaugT=vaug2,
              kaug=None, n2_sb=n2s2, n2col=16)
    wkv2_grp.close()
    wq2_grp = ExitStack()
    wqq2 = wq2_grp.enter_context(nc.sbuf_tensor(uname("wqq2"), [128, 8 * 1024], BF16))
    load_w_chunks(wqq2, wq2_h, 0, 1024)
    kvq_phase("q", False, ln1x_h, False, wqq2, 0, 1024, qtp=q2tp_h, n2_sb=n2s2, n2col=0)
    wq2_grp.close()
    norm_finalize(n2s2, 1)

    def pass2_packed():
        grp = ExitStack()
        qb2 = grp.enter_context(nc.sbuf_tensor(uname("qb2"), [128, 2 * 512], BF16))
        exps = grp.enter_context(nc.sbuf_tensor(uname("exps2"), [128, 4 * 1024], BF16))
        rcp = grp.enter_context(nc.sbuf_tensor(uname("rcp2"), [1, 512], BF16))
        rb = grp.enter_context(nc.sbuf_tensor(uname("rb2"), [64, 512], BF16))
        att = grp.enter_context(nc.sbuf_tensor(uname("att2"), [64, 2 * 1024], BF16))
        psc = grp.enter_context(nc.psum_tensor(uname("psc2"), [128, 2048], F32))
        pao = grp.enter_context(nc.psum_tensor(uname("pao2"), [128, 1536], F32))
        NKT = NT
        TQ = NKT + 2
        QPARS = [0, 1] if NQC >= 2 else [0]
        NQP = len(QPARS)
        NQI = NQC // NQP
        for f in range(8):
            base = dict(L)
            T0, A0, V0, G0, Q0, O0 = (base["cT"], base["cA"], base["cV"],
                                      base["cG"], base["sQ"], base["sO"])
            with nc.Block() as blk:
                @blk.sync
                def _(sync):
                    for qc in range(NQC):
                        qpar = qc % 2
                        if qc >= 2:
                            sync.wait_ge(cT, T0 + TQ * (qc - 1))
                        sync.dma_start(qb2[:, qpar * 512:(qpar + 1) * 512],
                                       AP(q2tp_h, f * 128 * S + qc * 512, [[S, 128], [1, 512]])
                                       ).then_inc(sQ, 16)
                        sync.wait_ge(cV, V0 + 6 * qc + 3)
                        sync.dma_start(AP(attnt2_h, (2 * f) * 64 * S + qc * 512,
                                          [[S, 64], [1, 512]]),
                                       att[:, qpar * 1024:qpar * 1024 + 512]).then_inc(sO, 16)
                        sync.wait_ge(cV, V0 + 6 * qc + 6)
                        sync.dma_start(AP(attnt2_h, (2 * f + 1) * 64 * S + qc * 512,
                                          [[S, 64], [1, 512]]),
                                       att[:, qpar * 1024 + 512:qpar * 1024 + 1024]
                                       ).then_inc(sO, 16)

                @blk.tensor
                def _(tensor):
                  with tensor.register(uname("rw")) as w:
                    with fori(tensor, NQI) as iq:
                        for qp in QPARS:
                            rwait(tensor, w, sQ, iq, 16 * NQP, qp * 16 + Q0 + 16)
                            for kt in range(NKT):
                                bk = kt % 2
                                rwait(tensor, w, cA, iq, NQP * NKT, qp * NKT + A0 + kt - 1)
                                if kt == 0:
                                    rwait(tensor, w, cV, iq, 6 * NQP, qp * 6 + V0)
                                tensor.matmul(psc[:, bk * 1024:bk * 1024 + 512],
                                              kts2[f][0:64, kt * 128:(kt + 1) * 128],
                                              qb2[0:64, qp * 512:(qp + 1) * 512],
                                              start=True, stop=True)
                                tensor.matmul(psc[:, bk * 1024 + 512:bk * 1024 + 1024],
                                              kts2[f][64:128, kt * 128:(kt + 1) * 128],
                                              qb2[64:128, qp * 512:(qp + 1) * 512],
                                              start=True, stop=True,
                                              tile_position=(64, 0)).then_inc(cT, 1)
                                if kt >= 2:
                                    ka = kt - 2
                                    rwait(tensor, w, cA, iq, NQP * NKT, qp * NKT + A0 + ka + 1)
                                    tensor.matmul(pao[0:65, 0:512],
                                                  vaug2[:, ka * 1040 + (2 * f) * 65:
                                                        ka * 1040 + (2 * f) * 65 + 65],
                                                  exps[:, (ka % 4) * 1024:(ka % 4) * 1024 + 512],
                                                  start=(ka == 0), stop=False)
                                    tensor.matmul(pao[0:65, 512:1024],
                                                  vaug2[:, ka * 1040 + (2 * f + 1) * 65:
                                                        ka * 1040 + (2 * f + 1) * 65 + 65],
                                                  exps[:, (ka % 4) * 1024 + 512:(ka % 4) * 1024 + 1024],
                                                  start=(ka == 0), stop=False).then_inc(cG, 2)
                            for ka in (NKT - 2, NKT - 1):
                                rwait(tensor, w, cA, iq, NQP * NKT, qp * NKT + A0 + ka + 1)
                                tensor.matmul(pao[0:65, 0:512],
                                              vaug2[:, ka * 1040 + (2 * f) * 65:
                                                    ka * 1040 + (2 * f) * 65 + 65],
                                              exps[:, (ka % 4) * 1024:(ka % 4) * 1024 + 512],
                                              start=False, stop=(ka == NKT - 1))
                                tensor.matmul(pao[0:65, 512:1024],
                                              vaug2[:, ka * 1040 + (2 * f + 1) * 65:
                                                    ka * 1040 + (2 * f + 1) * 65 + 65],
                                              exps[:, (ka % 4) * 1024 + 512:(ka % 4) * 1024 + 1024],
                                              start=False, stop=(ka == NKT - 1)).then_inc(cG, 2)
                            rwait(tensor, w, cV, iq, 6 * NQP, qp * 6 + V0 + 1)
                            tensor.matmul(pao[0:64, 1024:1536], onescol[:], rcp[:],
                                          start=True, stop=True).then_inc(cT, 1)
                            rwait(tensor, w, cV, iq, 6 * NQP, qp * 6 + V0 + 4)
                            tensor.matmul(pao[0:64, 1024:1536], onescol[:], rcp[:],
                                          start=True, stop=True).then_inc(cT, 1)

                @blk.scalar
                def _(scalar):
                  with scalar.register(uname("rw")) as w:
                    with fori(scalar, NQI) as iq:
                        for qp in QPARS:
                            for kt in range(NKT):
                                bk = kt % 2
                                rwait(scalar, w, cT, iq, NQP * TQ, qp * TQ + T0 + kt + 1)
                                rwait(scalar, w, cG, iq, NQP * 2 * NKT, qp * 2 * NKT + G0 + 2 * kt - 6)
                                scalar.activation(exps[:, (kt % 4) * 1024:(kt % 4) * 1024 + 512],
                                                  psc[:, bk * 1024:bk * 1024 + 512],
                                                  AF.Exp, bias=mb2[:, 2 * f:2 * f + 1],
                                                  scale=SCALE)
                                scalar.activation(exps[:, (kt % 4) * 1024 + 512:
                                                       (kt % 4) * 1024 + 1024],
                                                  psc[:, bk * 1024 + 512:bk * 1024 + 1024],
                                                  AF.Exp, bias=mb2[:, 2 * f + 1:2 * f + 2],
                                                  scale=SCALE).then_inc(cA, 1)

                @blk.vector
                def _(vector):
                  with vector.register(uname("rw")) as w:
                    with fori(vector, NQI) as iq:
                        for qp in QPARS:
                            rwait(vector, w, cG, iq, NQP * 2 * NKT, qp * 2 * NKT + G0 + 2 * NKT)
                            with nc.allow_low_precision(reason="bf16 softmax recip"):
                                vector.reciprocal(rcp[:], pao[64:65, 0:512]).then_inc(cV, 1)
                            rwait(vector, w, cT, iq, NQP * TQ, qp * TQ + T0 + NKT + 1)
                            vector.tensor_copy(rb[:], pao[0:64, 1024:1536]).then_inc(cV, 1)
                            rwait(vector, w, sO, iq, 32 * NQP, qp * 32 + O0 - 32)
                            vector.tensor_tensor(att[:, qp * 1024:qp * 1024 + 512],
                                                 pao[0:64, 0:512], rb[:],
                                                 ALU.mult).then_inc(cV, 1)
                            with nc.allow_low_precision(reason="bf16 softmax recip"):
                                vector.reciprocal(rcp[:], pao[64:65, 512:1024]).then_inc(cV, 1)
                            rwait(vector, w, cT, iq, NQP * TQ, qp * TQ + T0 + NKT + 2)
                            vector.tensor_copy(rb[:], pao[0:64, 1024:1536]).then_inc(cV, 1)
                            vector.tensor_tensor(att[:, qp * 1024 + 512:qp * 1024 + 1024],
                                                 pao[0:64, 512:1024], rb[:],
                                                 ALU.mult).then_inc(cV, 1)

                @blk.gpsimd
                def _(gpsimd):
                    for qc in range(0, NQC, 2):
                        gpsimd.wait_ge(cV, V0 + 6 * qc)
                        emit_casts(gpsimd, 3)

            L["sQ"] = Q0 + 16 * NQC
            L["sO"] = O0 + 32 * NQC
            L["cT"] = T0 + TQ * NQC
            L["cG"] = G0 + 2 * NKT * NQC
            L["cA"] = A0 + NKT * NQC
            L["cV"] = V0 + 6 * NQC
            nc.all_engine_barrier()
        grp.close()

    pass2_packed()
    s2.close()

    proj_phase("u", attnt2_h, wproj_h, None, x1_h, u_h, ln2w_h, ln2b_h, None)

    moe = ExitStack()
    w1m = moe.enter_context(nc.sbuf_tensor(uname("w1m"), [128, 8 * HID], BF16))
    w2m = moe.enter_context(nc.sbuf_tensor(uname("w2m"), [128, 2 * 8192], BF16))
    btc = moe.enter_context(nc.sbuf_tensor(uname("btc"), [128, 2 * 4096], BF16))
    gsb = moe.enter_context(nc.sbuf_tensor(uname("gsb"), [128, 32 * 512], BF16))
    usb = moe.enter_context(nc.sbuf_tensor(uname("usb"), [128, 2 * 4096], F32))
    mstage = moe.enter_context(nc.sbuf_tensor(uname("mstg"), [128, 2 * 4], F32))
    psm = moe.enter_context(nc.psum_tensor(uname("psm"), [128, 4096], F32))
    NC = NQC
    PARS = [0, 1] if NC >= 2 else [0]
    NPAR = len(PARS)
    NC2m = NC // NPAR
    moe_sO0 = L["sO"]
    for e in range(NE):
        base = dict(L)
        T0, A0, V0, G0, Q0, O0 = (base["cT"], base["cA"], base["cV"], base["cG"],
                                  base["sQ"], base["sO"])
        with nc.Block() as blk:
            @blk.sync
            def _(sync):
                sync.wait_ge(sOc, 16 * (e + 1))
                if e > 0:
                    sync.wait_ge(cT, T0)
                for dc in range(8):
                    sync.dma_start(w1m[:, dc * HID:(dc + 1) * HID],
                                   AP(w1b_h, (e * D + dc * 128) * HID,
                                      [[HID, 128], [1, HID]])).then_inc(sQ, 16)
                for c in range(NC):
                    par = c % 2
                    if c >= 2:
                        sync.wait_ge(cT, T0 + 32 * (c - 1))
                    sync.dma_start(btc[:, par * 4096:(par + 1) * 4096],
                                   AP(bt_h, c * 512, [[S, 128], [128 * S, 8], [1, 512]])
                                   ).then_inc(sQ, 16)
                    if e >= 1:
                        sync.wait_ge(sO, moe_sO0 + 16 * ((e - 1) * NC + c + 1))
                    sync.dma_start(usb[:, par * 4096:(par + 1) * 4096],
                                   AP(u_h, c * 512 * D, [[D, 128], [128 * D, 4], [1, D]])
                                   ).then_inc(sQ, 16)
                    for tt in range(4):
                        sync.dma_start(mstage[:, par * 4 + tt:par * 4 + tt + 1],
                                       AP(masks_h, e * S + c * 512 + tt * 128,
                                          [[1, 128], [1, 1]])).then_inc(sQ, 16)
                    for g4 in range(4):
                        if g4 >= 2:
                            sync.wait_ge(cG, G0 + 32 * c + 8 * g4 - 8)
                        elif c >= 1:
                            sync.wait_ge(cG, G0 + 32 * (c - 1) + 8 * (g4 + 2) - 8)
                        sync.dma_start(w2m[:, (g4 % 2) * 8192:(g4 % 2) * 8192 + 8192],
                                       AP(w2b_h, (e * HID + (g4 * 8) * 128) * D,
                                          [[D, 128], [128 * D, 8], [1, D]])
                                       ).then_inc(sQ, 16)
                    sync.wait_ge(cV, V0 + 5 * c + 5)
                    sync.dma_start(AP(u_h, c * 512 * D, [[D, 128], [128 * D, 4], [1, D]]),
                                   usb[:, par * 4096:(par + 1) * 4096]).then_inc(sO, 16)

            @blk.tensor
            def _(tensor):
              with tensor.register(uname("rw")) as w:
                with fori(tensor, NC2m) as ic:
                  for par in PARS:
                    rwait(tensor, w, sQ, ic, 160 * NPAR, par * 160 + Q0 + 128 + 16)
                    rwait(tensor, w, cV, ic, 5 * NPAR, par * 5 + V0)
                    for ht in range(32):
                        bk = ht % 4
                        rwait(tensor, w, cA, ic, 32 * NPAR, par * 32 + A0 + ht - 3)
                        for dc in range(8):
                            ins = tensor.matmul(
                                psm[:, bk * 512:(bk + 1) * 512],
                                w1m[:, dc * HID + ht * 128:dc * HID + (ht + 1) * 128],
                                btc[:, par * 4096 + dc * 512:par * 4096 + (dc + 1) * 512],
                                start=(dc == 0), stop=(dc == 7))
                        ins.then_inc(cT, 1)
                    rwait(tensor, w, cA, ic, 32 * NPAR, par * 32 + A0 + 32)
                    for ht in range(32):
                        g4 = ht // 8
                        rwait(tensor, w, sQ, ic, 160 * NPAR,
                              par * 160 + Q0 + 224 + 16 * (g4 + 1))
                        for tt in range(4):
                            for n in range(2):
                                ins = tensor.matmul(
                                    psm[:, (tt * 2 + n) * 512:(tt * 2 + n + 1) * 512],
                                    gsb[:, ht * 512 + tt * 128:ht * 512 + (tt + 1) * 128],
                                    w2m[:, (g4 % 2) * 8192 + (ht % 8) * 1024 + n * 512:
                                         (g4 % 2) * 8192 + (ht % 8) * 1024 + (n + 1) * 512],
                                    start=(ht == 0), stop=(ht == 31))
                        ins.then_inc(cG, 1)

            @blk.scalar
            def _(scalar):
              with scalar.register(uname("rw")) as w:
                with fori(scalar, NC2m) as ic:
                  for par in PARS:
                    for ht in range(32):
                        bk = ht % 4
                        rwait(scalar, w, cT, ic, 32 * NPAR, par * 32 + T0 + ht + 1)
                        rwait(scalar, w, cG, ic, 32 * NPAR, par * 32 + G0 + ht + 1 - 32)
                        scalar.activation(gsb[:, ht * 512:(ht + 1) * 512],
                                          psm[:, bk * 512:(bk + 1) * 512],
                                          AF.Gelu_apprx_tanh, bias=0.0, scale=1.0
                                          ).then_inc(cA, 1)

            @blk.vector
            def _(vector):
              with vector.register(uname("rw")) as w:
                with fori(vector, NC2m) as ic:
                  for par in PARS:
                    rwait(vector, w, cG, ic, 32 * NPAR, par * 32 + G0 + 32)
                    rwait(vector, w, sQ, ic, 160 * NPAR, par * 160 + Q0 + 224)
                    vector.memset(gsb[:, 0:1], 0.0).then_inc(cV, 1)
                    for tt in range(4):
                        uslc = usb[:, par * 4096 + tt * 1024:par * 4096 + (tt + 1) * 1024]
                        vector.scalar_tensor_tensor(
                            uslc, psm[:, tt * 1024:(tt + 1) * 1024],
                            mstage[:, par * 4 + tt:par * 4 + tt + 1],
                            uslc, ALU.mult, ALU.add).then_inc(cV, 1)

            @blk.gpsimd
            def _(gpsimd):
                if e == 0:
                    finish_casts(gpsimd)

        L["sQ"] = Q0 + 16 * (8 + NC * 10)
        L["sO"] = O0 + 16 * NC
        L["cT"] = T0 + 32 * NC
        L["cA"] = A0 + 32 * NC
        L["cG"] = G0 + 32 * NC
        L["cV"] = V0 + 5 * NC
        nc.all_engine_barrier()
    moe.close()

    fin = ExitStack()
    fx1 = fin.enter_context(nc.sbuf_tensor(uname("fx1"), [128, 2 * 1024], F32))
    fu = fin.enter_context(nc.sbuf_tensor(uname("fu"), [128, 2 * 1024], F32))
    fxa = fin.enter_context(nc.sbuf_tensor(uname("fxa"), [128, 1024], F32))
    fxo = fin.enter_context(nc.sbuf_tensor(uname("fxo"), [128, 2 * 1024], F32))
    fsq = fin.enter_context(nc.sbuf_tensor(uname("fsq"), [128, 1024], BF16))
    fst = fin.enter_context(nc.sbuf_tensor(uname("fst"), [128, 8], F32))
    base = dict(L)
    T0, A0, V0, L0, O0 = base["cT"], base["cA"], base["cV"], base["sL"], base["sO"]
    G0f = base["cG"]
    with nc.Block() as blk:
        @blk.sync
        def _(sync):
            for t in range(NT):
                par = t % 2
                if t >= 2:
                    sync.wait_ge(cV, V0 + 5 * (t - 1))
                sync.dma_start(fx1[:, par * 1024:(par + 1) * 1024],
                               AP(x1_h, t * 128 * D, [[D, 128], [1, D]])).then_inc(sL, 16)
                sync.dma_start(fu[:, par * 1024:(par + 1) * 1024],
                               AP(u_h, t * 128 * D, [[D, 128], [1, D]])).then_inc(sL, 16)
                sync.wait_ge(cV, V0 + 5 * t + 5)
                sync.dma_start(AP(y_h, t * 128 * D, [[D, 128], [1, D]]),
                               fxo[:, par * 1024:(par + 1) * 1024]).then_inc(sO, 16)
            sync.wait_ge(sO, O0 + 16 * NT)

        @blk.vector
        def _(vector):
          with vector.register(uname("rw")) as w:
            with fori(vector, NT2) as it:
                for par in range(2):
                    rwait(vector, w, sL, it, 64, par * 32 + L0 + 32)
                    vector.tensor_add(fxa[:], fx1[:, par * 1024:(par + 1) * 1024],
                                      fu[:, par * 1024:(par + 1) * 1024])
                    vector.tensor_reduce(fst[:, 0:1], fxa[:], AX.X, ALU.add,
                                         negate=True).then_inc(cV, 1)
                    rwait(vector, w, cV, it, 10, par * 5 + V0 + 1)
                    vector.tensor_mul(fst[:, 1:2], fst[:, 0:1],
                                      invd[:]).then_inc(cG, 1)
                    rwait(vector, w, cG, it, 4, par * 2 + G0f + 1)
                    vector.tensor_mul(fst[:, 2:3], fst[:, 1:2],
                                      fst[:, 1:2]).then_inc(cV, 1)
                    rwait(vector, w, cA, it, 4, par * 2 + A0 + 1)
                    vector.tensor_mul(fst[:, 0:1], fst[:, 3:4],
                                      invd[:]).then_inc(cG, 1)
                    rwait(vector, w, cG, it, 4, par * 2 + G0f + 2)
                    rwait(vector, w, cV, it, 10, par * 5 + V0 + 2)
                    vector.tensor_sub(fst[:, 4:5], fst[:, 0:1],
                                      fst[:, 2:3]).then_inc(cV, 1)
                    rwait(vector, w, cA, it, 4, par * 2 + A0 + 2)
                    vector.reciprocal(fst[:, 6:7], fst[:, 5:6]).then_inc(cV, 1)
                    rwait(vector, w, cV, it, 10, par * 5 + V0 + 4)
                    vector.tensor_scalar(fxo[:, par * 1024:(par + 1) * 1024],
                                         fxa[:], fst[:, 1:2], fst[:, 6:7],
                                         ALU.add, ALU.mult).then_inc(cV, 1)

        @blk.scalar
        def _(scalar):
          with scalar.register(uname("rw")) as w:
            with fori(scalar, NT2) as it:
                for par in range(2):
                    rwait(scalar, w, cV, it, 10, par * 5 + V0 + 1)
                    scalar.activation(fsq[:], fxa[:], AF.Square, bias=0.0, scale=1.0,
                                      accum_out=fst[:, 3:4]).then_inc(cA, 1)
                    rwait(scalar, w, cV, it, 10, par * 5 + V0 + 3)
                    scalar.activation(fst[:, 5:6], fst[:, 4:5], AF.Sqrt, bias=epsc[:, 0:1],
                                      scale=1.0).then_inc(cA, 1)
    L["sL"] = L0 + 32 * NT
    L["sO"] = O0 + 16 * NT
    L["cV"] = V0 + 5 * NT
    L["cA"] = A0 + 2 * NT
    L["cG"] = G0f + 2 * NT
    return nc


def _host_prep(inputs, S):
    i = inputs
    wq1t = np.ascontiguousarray(np.asarray(i["L0_attn_w_qkv"]).T)
    woutt = np.ascontiguousarray(np.asarray(i["L0_attn_w_out"]).T)
    inv_freq = (1.0 / np.float32(10000.0) **
                (np.arange(0, HD, 2, dtype=np.float32) / np.float32(HD))).astype(np.float32)
    t = np.arange(S, dtype=np.float32)
    freqs = t[:, None] * inv_freq[None, :]
    cosr = np.tile(np.cos(freqs).astype(np.float32), (1, 16)).astype(BF)
    sinr = np.tile(np.sin(freqs).astype(np.float32), (1, 16)).astype(BF)
    ident = np.eye(128, dtype=np.float32).astype(BF)
    com = {
        "wq1t": wq1t, "woutt": woutt,
        "bout": np.asarray(i["L0_attn_b_out"]).reshape(1, D),
        "ln1w": np.asarray(i["L0_ffn_ln1_w"]).reshape(1, D),
        "ln1b": np.asarray(i["L0_ffn_ln1_b"]).reshape(1, D),
        "ln2w": np.asarray(i["L0_ffn_ln2_w"]).reshape(1, D),
        "ln2b": np.asarray(i["L0_ffn_ln2_b"]).reshape(1, D),
        "wq2": np.asarray(i["L0_ffn_w_qkv"]),
        "wproj": np.asarray(i["L0_ffn_w_proj"]),
        "gatew": np.asarray(i["L0_ffn_gate_w"]),
        "gateb": np.asarray(i["L0_ffn_gate_b"]).reshape(1, NE),
        "w1": np.asarray(i["L0_ffn_w1"]),
        "w2": np.asarray(i["L0_ffn_w2"]),
        "cosr": cosr, "sinr": sinr, "identh": ident,
    }
    com = {k: np.ascontiguousarray(v) for k, v in com.items()}
    return com


def kernel(**inputs):
    x = np.asarray(inputs["L0_attn_x"])
    B, S, _ = x.shape
    com = _host_prep(inputs, S)
    nc = build_nc(S)
    in_maps = [dict(com, x=np.ascontiguousarray(x[c])) for c in range(B)]
    res = run_bass_kernel_spmd(nc, in_maps, core_ids=list(range(B)))
    return np.stack([r["y"] for r in res.results]).astype(np.float32)
